# version 07
# baseline (speedup 1.0000x reference)
import numpy as np
import ml_dtypes
import concourse.bass as bass
import concourse.mybir as mybir

F32 = mybir.dt.float32
BF16 = mybir.dt.bfloat16
AF = mybir.ActivationFunctionType
ALU = mybir.AluOpType
AX = mybir.AxisListType

BF = ml_dtypes.bfloat16
FENCE_L0AA = False
L0AA_PARTS = 7
L0AA_DIVMODE = 0
T = 2048
D = 1024
NB = 4
NTT = T // 128
NK = D // 128


def host_constants():
    d = np.arange(1024, dtype=np.float64)
    ang = 2 * np.pi * np.outer(d, d) / 1024.0
    C1024 = np.cos(ang).astype(np.float32)
    S1024 = np.sin(ang).astype(np.float32)
    n = np.arange(128, dtype=np.float64)
    ang2 = 2 * np.pi * np.outer(n, n) / 128.0
    C128 = np.cos(ang2).astype(np.float32)
    S128 = np.sin(ang2).astype(np.float32)
    eye = np.eye(128, dtype=np.float32)
    i_idx = np.arange(512)[None, :]
    j_idx = np.arange(128)[:, None]
    masks = np.stack([(i_idx - 128 * r - j_idx >= 0) for r in range(4)]).astype(np.float32)
    d0 = (i_idx - j_idx).astype(np.float32) * np.ones((128, 1), np.float32)
    deltas = np.stack([d0] + [np.maximum(d0 - 128 * r, 0) for r in range(4)])
    cons = {
        "cC1024": C1024.astype(BF), "cS1024": S1024.astype(BF), "cS1024n": (-S1024).astype(BF),
        "cC128": C128.astype(BF), "cS128": S128.astype(BF), "cS128n": (-S128).astype(BF),
        "cEye": eye.astype(BF),
        "cEyeF": eye,
        "cMasks": masks.astype(BF),
        "cDeltas": deltas.astype(np.float16),
        "cOffs": (-384.0 + 128.0 * np.arange(20, dtype=np.float32)).reshape(1, 20),
    }
    return cons



class SC:
    def __init__(self, nc, name):
        self.h = nc.alloc_semaphore(name)
        self.n = 0
    def inc(self, instr, k=1):
        instr.then_inc(self.h, k)
        self.n += k
        return instr
    def wait(self, eng, val=None):
        eng.wait_ge(self.h, self.n if val is None else val)


class Bank:
    def __init__(self, nc, name, shape=(128, 512), dtype=F32):
        self.t = nc.alloc_psum_tensor(name, list(shape), dtype)
        self._sc = SC(nc, name + "_f")
        self.rel = None
        self.pending = False
    def begin(self, eng):
        assert not self.pending, "bank not released"
        if self.rel is not None:
            eng.wait_ge(self.rel[0], self.rel[1])
        self.pending = True
        return self.t
    def end(self, instr, sc=None):
        assert self.pending
        sc = sc or self._sc
        sc.inc(instr)
        self.rel = (sc.h, sc.n)
        self.pending = False
        return instr
    def release_at(self, sc):
        assert self.pending
        self.rel = (sc.h, sc.n)
        self.pending = False


class Ctx:
    def __init__(self, nc, cons_shapes=True):
        self.nc = nc
        self.t = nc.tensor
        self.v = nc.vector
        self.s = nc.scalar
        self.g = nc.gpsimd
        self.sp = nc.sync
        self._sc = {}
        self.cons = {}
        for name, shp, dt in [("cC1024", [1024, 1024], BF16), ("cS1024", [1024, 1024], BF16),
                              ("cS1024n", [1024, 1024], BF16), ("cC128", [128, 128], BF16),
                              ("cS128", [128, 128], BF16), ("cS128n", [128, 128], BF16),
                              ("cEye", [128, 128], BF16), ("cEyeF", [128, 128], F32),
                              ("cMasks", [4, 128, 512], BF16),
                              ("cDeltas", [5, 128, 512], mybir.dt.float16),
                              ("cOffs", [1, 20], F32)]:
            self.cons[name] = nc.dram_tensor(name, shp, dt, kind="ExternalInput").ap()
        self.eye = nc.alloc_sbuf_tensor("eye", [128, 128], BF16)
        self.eye_f = nc.alloc_sbuf_tensor("eye_f", [128, 128], F32)
        self.eps = nc.alloc_sbuf_tensor("eps", [128, 1], F32)
        self.halfpi = nc.alloc_sbuf_tensor("halfpi", [128, 1], F32)
        self.onesf = nc.alloc_sbuf_tensor("onesf", [128, 1], F32)
        self.xarena = nc.alloc_sbuf_tensor("xarena", [128, 2048], F32)
        self.lns = nc.alloc_sbuf_tensor("lns", [128, 96], F32)
        self.x_r = [nc.alloc_sbuf_tensor(f"x_r{i}", [128, 1024], F32) for i in range(2)]
        self.sq_tr = nc.alloc_sbuf_tensor("sq_tr", [128, 1024], BF16)
        self.stat = nc.alloc_sbuf_tensor("stat", [128, 8], F32)
        self.xn_tm = nc.alloc_sbuf_tensor("xn_tm", [128, 1024], BF16)
        self.xn = nc.alloc_sbuf_tensor("xn", [128, NK * T], BF16)
        self.y_tmp = nc.alloc_sbuf_tensor("y_tmp", [128, 4096], BF16)
        self.pb = [Bank(nc, f"pb{i}") for i in range(4)]
        self.hold = [Bank(nc, f"ph{i}") for i in range(2)]
        self.trb = [Bank(nc, f"trb{i}", (128, 1024), BF16) for i in range(2)]
        self._pb_i = 0
    def bank(self):
        b = self.pb[self._pb_i % len(self.pb)]
        self._pb_i += 1
        return b
    def sc(self, name):
        if name not in self._sc:
            self._sc[name] = SC(self.nc, name)
        return self._sc[name]

    def fence_all(self):
        for e in (self.t, self.v, self.s, self.sp, self.g):
            for sc in self._sc.values():
                if sc.n:
                    e.wait_ge(sc.h, sc.n)

    def begin_phase(self, prefix):
        import contextlib
        self._stack = contextlib.ExitStack()
        self._prefix = prefix

    def alloc(self, name, shape, dtype):
        if getattr(self, "_stack", None) is not None:
            return self._stack.enter_context(self.nc.sbuf_tensor(self._prefix + name, shape, dtype))
        return self.nc.alloc_sbuf_tensor(name, shape, dtype)

    def end_phase(self):
        if getattr(self, "_stack", None) is not None:
            self._stack.close()
            self._stack = None

    def emit_const_init(self):
        c = self.sc("consts")
        c.inc(self.sp.dma_start(out=self.eye[:], in_=self.cons["cEye"]), 16)
        c.inc(self.sp.dma_start(out=self.eye_f[:], in_=self.cons["cEyeF"]), 16)
        self.g.memset(self.eps[:], 1e-5)
        self.g.memset(self.halfpi[:], float(np.pi / 2))
        c.inc(self.g.memset(self.onesf[:], 1.0))
        for e in (self.t, self.v, self.s, self.sp):
            c.wait(e)


def emit_ln(ctx, x_hbm, b):
    nc, t, v, s, sp, g = ctx.nc, ctx.t, ctx.v, ctx.s, ctx.sp, ctx.g
    ld = ctx.sc("ln_ld")
    vst = ctx.sc("ln_v")
    sst = ctx.sc("ln_s")
    trc = ctx.sc("ln_tr")
    hp = ctx.sc("ln_hop")
    L = ctx.lns
    SUM, SSQ, MEAN, RSTD, NMR, NM2, VAR = 0, 16, 32, 48, 64, 80, 88
    for half in range(8):
        c0 = half * 2
        sp.wait_ge(sst.h, sst.n)
        sp.wait_ge(vst.h, vst.n)
        last_red = None
        for j in range(2):
            tt = c0 + j
            xt = ctx.xarena[:, j * 1024:(j + 1) * 1024]
            ld.inc(sp.dma_start(out=xt, in_=x_hbm[b, tt * 128:(tt + 1) * 128, :]), 16)
            ld.wait(v)
            last_red = v.tensor_reduce(L[:, SUM + tt:SUM + tt + 1], xt, AX.X, ALU.add)
            ld.wait(s)
            vst.wait(s)
            sst.inc(s.activation(ctx.sq_tr[:], xt, AF.Square))
            sst.wait(v)
            last_red = v.tensor_reduce(L[:, SSQ + tt:SSQ + tt + 1], ctx.sq_tr[:], AX.X, ALU.add)
            vst.inc(last_red)
        vst.wait(v)
        sst.wait(v)
        ms = slice(MEAN + c0, MEAN + c0 + 2)
        hp.inc(v.tensor_scalar(L[:, ms], L[:, SUM + c0:SUM + c0 + 2], 1.0 / D, None, ALU.mult))
        hp.wait(v)
        hp.inc(v.scalar_tensor_tensor(L[:, NM2:NM2 + 2], L[:, ms], -1.0, L[:, ms], ALU.mult, ALU.mult))
        hp.wait(v)
        vst.inc(v.scalar_tensor_tensor(L[:, VAR:VAR + 2], L[:, SSQ + c0:SSQ + c0 + 2], 1.0 / D,
                                       L[:, NM2:NM2 + 2], ALU.mult, ALU.add))
        vst.wait(s)
        s.activation(L[:, NM2:NM2 + 2], L[:, VAR:VAR + 2], AF.Ln, bias=ctx.eps[:])
        sst.inc(s.activation(L[:, RSTD + c0:RSTD + c0 + 2], L[:, NM2:NM2 + 2], AF.Exp, scale=-0.5))
        sst.wait(v)
        vst.inc(v.scalar_tensor_tensor(L[:, NMR + c0:NMR + c0 + 2], L[:, ms], -1.0,
                                       L[:, RSTD + c0:RSTD + c0 + 2], ALU.mult, ALU.mult))
        vst.wait(s)
        for j in range(2):
            tt = c0 + j
            xt = ctx.xarena[:, j * 1024:(j + 1) * 1024]
            trc.wait(s)
            sst.inc(s.activation(ctx.xn_tm[:], xt, AF.Identity,
                                 bias=L[:, NMR + tt:NMR + tt + 1], scale=L[:, RSTD + tt:RSTD + tt + 1]))
            trb = ctx.trb[tt % 2]
            sst.wait(t)
            ps = trb.begin(t)
            for k in range(NK):
                ins = t.transpose(ps[:, k * 128:(k + 1) * 128], ctx.xn_tm[:, k * 128:(k + 1) * 128], ctx.eye[:])
            trc.inc(ins)
            trc.wait(v)
            dst = ctx.xn[:].rearrange("p (k t) -> p k t", t=T)[:, :, tt * 128: tt * 128 + 128]
            src3 = ps[:].rearrange("p (k c) -> p k c", c=128)
            trb.end(v.tensor_copy(dst, src3), trc)


def emit_finale_tg(ctx, x_hbm, y_hbm, b, tg, y_fm, bias_tile=None):
    nc, t, v, s, sp = ctx.nc, ctx.t, ctx.v, ctx.s, ctx.sp
    ld = ctx.sc("fin_ld")
    fv = ctx.sc("fin_v")
    yw = ctx.sc("ystore")
    for ts in range(4):
        tok0 = tg * 512 + ts * 128
        xr = ctx.x_r[ts % 2]
        sp.wait_ge(fv.h, fv.n)
        sp.wait_ge(yw.h, yw.n)
        ld.inc(sp.dma_start(out=xr[:], in_=x_hbm[b, tok0:tok0 + 128, :]), 16)
        trb = ctx.trb[ts % 2]
        ps = trb.begin(t)
        for ot in range(NK):
            ins = t.transpose(ps[:, ot * 128:(ot + 1) * 128], y_fm[:, ot * 512 + ts * 128: ot * 512 + ts * 128 + 128], ctx.eye[:])
        tr = ctx.sc("fin_tr")
        tr.inc(ins)
        ld.wait(v)
        tr.wait(v)
        pst = ps[:].rearrange("p (k c) -> p k c", c=128)
        xr3 = xr[:].rearrange("p (k c) -> p k c", c=128)
        trb.end(v.tensor_tensor(xr3, xr3, pst, ALU.add), fv)
        fv.wait(sp)
        yw.inc(sp.dma_start(out=y_hbm[b, tok0:tok0 + 128, :], in_=xr[:]), 16)
    tr.wait(s)


def emit_finale_tg32(ctx, x_hbm, y_hbm, b, tg, y_fm32):
    nc, t, v, s, sp = ctx.nc, ctx.t, ctx.v, ctx.s, ctx.sp
    ld = ctx.sc("fin_ld")
    fv = ctx.sc("fin_v")
    yw = ctx.sc("ystore")
    tr = ctx.sc("fin_tr")
    for ts in range(4):
        tok0 = tg * 512 + ts * 128
        xr = ctx.x_r[ts % 2]
        sp.wait_ge(fv.h, fv.n)
        sp.wait_ge(yw.h, yw.n)
        ld.inc(sp.dma_start(out=xr[:], in_=x_hbm[b, tok0:tok0 + 128, :]), 16)
        for half in range(2):
            bk = ctx.bank()
            ps = bk.begin(t)
            for oi in range(4):
                ot = half * 4 + oi
                ins = t.transpose(ps[:, oi * 128:(oi + 1) * 128],
                                  y_fm32[:, ot * 512 + ts * 128: ot * 512 + ts * 128 + 128],
                                  ctx.eye_f[:])
            tr.inc(ins)
            ld.wait(v)
            tr.wait(v)
            pst = ps[:].rearrange("p (k c) -> p k c", c=128)
            xr3 = xr[:, half * 512:(half + 1) * 512].rearrange("p (k c) -> p k c", c=128)
            bk.end(v.tensor_tensor(xr3, xr3, pst, ALU.add), fv)
        fv.wait(sp)
        yw.inc(sp.dma_start(out=y_hbm[b, tok0:tok0 + 128, :], in_=xr[:]), 16)
    ctx.sc("fin_tr").wait(s)



def emit_l1ffn_weights(ctx, wt, cache=None, mode="build"):
    nc, v, s, sp, g = ctx.nc, ctx.v, ctx.s, ctx.sp, ctx.g
    ctx.cosT = ctx.alloc("cosT", [128, 8192], BF16)
    ctx.sinP = ctx.alloc("sinP", [128, 8192], BF16)
    ctx._e1 = ctx.alloc("e1", [128, 512], F32)
    ctx._e2 = ctx.alloc("e2", [128, 512], F32)
    ctx._e3 = ctx.alloc("e3", [128, 512], F32)
    ctx._e4 = ctx.alloc("e4", [128, 512], F32)
    ld = ctx.sc("w_ld")
    if mode == "load":
        ld.inc(sp.dma_start(out=ctx.cosT[:], in_=cache[0]), 16)
        ld.inc(sp.dma_start(out=ctx.sinP[:], in_=cache[1]), 16)
        for e in (ctx.t, ctx.s, ctx.v):
            ld.wait(e)
        return
    th = ctx.alloc("th_t", [128, 1024], F32)
    tn = ctx.alloc("tn_t", [128, 1024], F32)
    zz = ctx.alloc("zz_t", [128, 1024], F32)
    wv = ctx.sc("w_v")
    ws = ctx.sc("w_s")
    for (w0, w1, dst, use_bias) in [(wt["theta"], wt["theta_noise"], ctx.cosT, True),
                                    (wt["phi"], wt["phi_noise"], ctx.sinP, False)]:
        for kt in range(8):
            ws.wait(sp)
            ld.inc(sp.dma_start(out=th[:], in_=w0[kt * 128:(kt + 1) * 128, :]), 16)
            ld.inc(sp.dma_start(out=tn[:], in_=w1[kt * 128:(kt + 1) * 128, :]), 16)
            ld.wait(ctx.v)
            wv.inc(ctx.v.tensor_tensor(zz[:], th[:], tn[:], ALU.add))
            wv.wait(ctx.s)
            ins = ctx.s.activation(dst[:, kt * 1024:(kt + 1) * 1024], zz[:], AF.Sin,
                                   bias=(ctx.halfpi[:] if use_bias else 0.0))
            ws.inc(ins)
    if cache is not None:
        ws.wait(sp)
        ld.inc(sp.dma_start(out=cache[0], in_=ctx.cosT[:]), 16)
        ld.inc(sp.dma_start(out=cache[1], in_=ctx.sinP[:]), 16)
    for e in (ctx.t, ctx.s, ctx.v):
        ld.wait(e)
        ctx.sc("w_s").wait(e)


def emit_l1ffn_b(ctx, x_hbm, y_hbm, b):
    nc, t, v, s = ctx.nc, ctx.t, ctx.v, ctx.s
    e1, e2, e3, e4 = ctx._e1, ctx._e2, ctx._e3, ctx._e4
    mm = ctx.sc("ffn_mm")
    se = ctx.sc("ffn_s")
    ve = ctx.sc("ffn_v")
    for tg in range(4):
        for ot in range(8):
            se.wait(t)
            prb = ctx.bank()
            pr = prb.begin(t)
            for kt in range(8):
                ins = t.matmul(pr[:], ctx.cosT[:, kt * 1024 + ot * 128: kt * 1024 + ot * 128 + 128],
                               ctx.xn[:, kt * T + tg * 512: kt * T + tg * 512 + 512],
                               start=(kt == 0), stop=(kt == 7))
            mm.inc(ins)
            pib = ctx.bank()
            pi = pib.begin(t)
            for kt in range(8):
                ins = t.matmul(pi[:], ctx.sinP[:, kt * 1024 + ot * 128: kt * 1024 + ot * 128 + 128],
                               ctx.xn[:, kt * T + tg * 512: kt * T + tg * 512 + 512],
                               start=(kt == 0), stop=(kt == 7))
            mm.inc(ins)
            mm.wait(s)
            for bank, psv, dst in [(prb, pr, e1), (pib, pi, e2)]:
                ins = s.activation(dst[:], psv[:], AF.Square)
                bank.end(ins, se)
            se.wait(v)
            ve.inc(v.tensor_tensor(e3[:], e1[:], e2[:], ALU.add))
            ve.wait(s)
            s.activation(e4[:], e3[:], AF.Ln)
            ins = s.activation(ctx.y_tmp[:, ot * 512:(ot + 1) * 512], e4[:], AF.Exp, scale=0.5)
            se.inc(ins)
        se.wait(t)
        se.wait(v)
        emit_finale_tg(ctx, x_hbm, y_hbm, b, tg, ctx.y_tmp)


def build_l1ffn_program(debug=False):
    nc = bass.Bass("TRN2", target_bir_lowering=False, debug=False)
    x_in = nc.dram_tensor("x_in", [NB, T, D], F32, kind="ExternalInput").ap()
    x_out = nc.dram_tensor("x_out", [NB, T, D], F32, kind="ExternalOutput").ap()
    wt = {}
    for name in ["theta", "phi", "theta_noise", "phi_noise"]:
        wt[name] = nc.dram_tensor(name, [1024, 1024], F32, kind="ExternalInput").ap()
    ctx = Ctx(nc)
    with nc.Block("init", no_gpsimd_drain=True) as blk:
        @blk.sync
        def _(sp):
            ctx.emit_const_init()
            ctx.sc("consts").wait(ctx.s)
            ctx.sc("ginit").wait(ctx.s)
            ctx.sc("consts").wait(ctx.v)
            ctx.sc("ginit").wait(ctx.v)
            ctx.sc("ginit").wait(ctx.sp)
            emit_l1ffn_weights(ctx, wt)
    for b in range(NB):
        with nc.Block(f"b{b}", no_gpsimd_drain=True) as blk:
            @blk.sync
            def _(sp, b=b):
                emit_ln(ctx, x_in, b)
                ctx.sc("ln_s").wait(ctx.t)
                ctx.sc("ln_tr").wait(ctx.t)
                emit_l1ffn_b(ctx, x_in, x_out, b)
    if debug:
        dbg = {}
        for nm, tns in [("d_lng", ctx.lng), ("d_loff", ctx.loff), ("d_QF", ctx.QF), ("d_KF", ctx.KF),
                        ("d_VF", ctx.VF), ("d_gbuf", ctx.gbuf), ("d_AG", ctx.AG), ("d_xn", ctx.xn),
                        ("d_WTq", ctx.WT["q"]), ("d_WTv", ctx.WT["v"]), ("d_WOT", ctx.WOT)]:
            dbg[nm] = nc.dram_tensor(nm, list(tns.shape), tns.dtype, kind="ExternalOutput").ap()
        with nc.Block("dbg", no_gpsimd_drain=True) as blk:
            @blk.sync
            def _(sp):
                dd = ctx.sc("dbgst")
                for nm, tns in [("d_lng", ctx.lng), ("d_loff", ctx.loff), ("d_QF", ctx.QF), ("d_KF", ctx.KF),
                                ("d_VF", ctx.VF), ("d_gbuf", ctx.gbuf), ("d_AG", ctx.AG), ("d_xn", ctx.xn),
                                ("d_WTq", ctx.WT["q"]), ("d_WTv", ctx.WT["v"]), ("d_WOT", ctx.WOT)]:
                    dd.inc(ctx.sp.dma_start(out=dbg[nm], in_=tns[:]), 16)
                dd.wait(ctx.sp)
    with nc.Block("fin", no_gpsimd_drain=True) as blk:
        @blk.sync
        def _(sp):
            ctx.sc("ystore").wait(ctx.sp)
    return nc


def load_bias_cols(ctx, name, b_hbm):
    bt = ctx.alloc(name, [128, 8], F32)
    ld = ctx.sc("bias_ld")
    for k in range(8):
        src = b_hbm[k * 128:(k + 1) * 128].rearrange("(p one) -> p one", one=1)
        ld.inc(ctx.sp.dma_start(out=bt[:, k:k + 1], in_=src), 16)
    return bt


def emit_build_A(ctx, wr_hbm, wi_hbm, cache=None, mode="build"):
    nc, t, v, s, sp, g = ctx.nc, ctx.t, ctx.v, ctx.s, ctx.sp, ctx.g
    ctx.Ab = ctx.alloc("Ab", [128, 8192], BF16)
    ld = ctx.sc("A_ld")
    mm = ctx.sc("A_mm")
    ev = ctx.sc("A_ev")
    ng = ctx.sc("A_ng")
    if mode == "load":
        ld.inc(ctx.sp.dma_start(out=ctx.Ab[:], in_=cache), 16)
        ld.wait(ctx.t)
        ld.wait(ctx.s)
        ld.wait(ctx.v)
        return
    with (nc.sbuf_tensor("Cb", [128, 8192], BF16) as Cb,
          nc.sbuf_tensor("Sb", [128, 8192], BF16) as Sb,
          nc.sbuf_tensor("PT", [128, 8192], BF16) as PT,
          nc.sbuf_tensor("QT", [128, 8192], BF16) as QT,
          nc.sbuf_tensor("WR", [128, 8192], BF16) as WR,
          nc.sbuf_tensor("WI", [128, 8192], BF16) as WI):
        for kt in range(8):
            sl = slice(kt * 1024, (kt + 1) * 1024)
            hb = slice(kt * 128, (kt + 1) * 128)
            ld.inc(sp.dma_start(out=Cb[:, sl], in_=ctx.cons["cC1024"][hb, :]), 16)
            ld.inc(sp.dma_start(out=Sb[:, sl], in_=ctx.cons["cS1024"][hb, :]), 16)
            ld.inc(g.dma_start(out=WR[:, sl], in_=wr_hbm[hb, :]), 16)
            ld.inc(g.dma_start(out=WI[:, sl], in_=wi_hbm[hb, :]), 16)
        ld.wait(t)
        def stage(dst, terms, scale):
            for ot in range(8):
                for dg in range(2):
                    bk = ctx.bank()
                    ps = bk.begin(t)
                    nmm = len(terms) * 8
                    i = 0
                    for (lh, rh) in terms:
                        for ft in range(8):
                            ins = t.matmul(ps[:], lh[:, ft * 1024 + ot * 128: ft * 1024 + ot * 128 + 128],
                                           rh[:, ft * 1024 + dg * 512: ft * 1024 + dg * 512 + 512],
                                           start=(i == 0), stop=(i == nmm - 1))
                            i += 1
                    mm.inc(ins)
                    mm.wait(s)
                    bk.end(s.activation(dst[:, ot * 1024 + dg * 512: ot * 1024 + dg * 512 + 512],
                                        ps[:], AF.Identity, scale=scale), ev)
        stage(PT, [(WR, Cb), (WI, Sb)], 1.0)
        mm.wait(v)
        ng.inc(v.tensor_scalar(WI[:], WI[:], -1.0, None, ALU.mult))
        ng.wait(t)
        stage(QT, [(WR, Sb), (WI, Cb)], 1.0)
        ev.wait(t)
        stage(ctx.Ab, [(PT, Cb), (QT, Sb)], 1.0 / 1024.0)
        ev.wait(t)
        ev.wait(v)
        ev.wait(sp)
        ev.wait(g)
    if cache is not None:
        ev.wait(sp)
        ld.inc(sp.dma_start(out=cache, in_=ctx.Ab[:]), 16)
        ld.wait(sp)


def emit_l0ffn_b(ctx, x_hbm, y_hbm, b, bias_t):
    nc, t, v, s = ctx.nc, ctx.t, ctx.v, ctx.s
    mm = ctx.sc("ffn_mm")
    se = ctx.sc("ffn_s")
    for tg in range(4):
        for ot in range(8):
            bk = ctx.bank()
            ps = bk.begin(t)
            for kt in range(8):
                ins = t.matmul(ps[:], ctx.Ab[:, kt * 1024 + ot * 128: kt * 1024 + ot * 128 + 128],
                               ctx.xn[:, kt * T + tg * 512: kt * T + tg * 512 + 512],
                               start=(kt == 0), stop=(kt == 7))
            mm.inc(ins)
            mm.wait(s)
            bk.end(s.activation(ctx.y_tmp[:, ot * 512:(ot + 1) * 512], ps[:], AF.Identity,
                                bias=bias_t[:, ot:ot + 1]), se)
        se.wait(t)
        se.wait(v)
        emit_finale_tg(ctx, x_hbm, y_hbm, b, tg, ctx.y_tmp)


def build_l0ffn_program(debug=False):
    nc = bass.Bass("TRN2", target_bir_lowering=False, debug=False)
    x_in = nc.dram_tensor("x_in", [NB, T, D], F32, kind="ExternalInput").ap()
    x_out = nc.dram_tensor("x_out", [NB, T, D], F32, kind="ExternalOutput").ap()
    wr = nc.dram_tensor("w_freq_real", [1024, 1024], F32, kind="ExternalInput").ap()
    wi = nc.dram_tensor("w_freq_imag", [1024, 1024], F32, kind="ExternalInput").ap()
    bias = nc.dram_tensor("bias", [1024], F32, kind="ExternalInput").ap()
    ctx = Ctx(nc)
    with nc.Block("init", no_gpsimd_drain=True) as blk:
        @blk.sync
        def _(sp):
            ctx.emit_const_init()
            bias_t = load_bias_cols(ctx, "bias_t", bias)
            ctx.bias_t = bias_t
            emit_build_A(ctx, wr, wi)
            ctx.sc("bias_ld").wait(ctx.s)
    for b in range(NB):
        with nc.Block(f"b{b}", no_gpsimd_drain=True) as blk:
            @blk.sync
            def _(sp, b=b):
                emit_ln(ctx, x_in, b)
                ctx.sc("ln_s").wait(ctx.t)
                ctx.sc("ln_tr").wait(ctx.t)
                emit_l0ffn_b(ctx, x_in, x_out, b, ctx.bias_t)
    if debug:
        dbg = {}
        for nm, tns in [("d_lng", ctx.lng), ("d_loff", ctx.loff), ("d_QF", ctx.QF), ("d_KF", ctx.KF),
                        ("d_VF", ctx.VF), ("d_gbuf", ctx.gbuf), ("d_AG", ctx.AG), ("d_xn", ctx.xn),
                        ("d_WTq", ctx.WT["q"]), ("d_WTv", ctx.WT["v"]), ("d_WOT", ctx.WOT)]:
            dbg[nm] = nc.dram_tensor(nm, list(tns.shape), tns.dtype, kind="ExternalOutput").ap()
        with nc.Block("dbg", no_gpsimd_drain=True) as blk:
            @blk.sync
            def _(sp):
                dd = ctx.sc("dbgst")
                for nm, tns in [("d_lng", ctx.lng), ("d_loff", ctx.loff), ("d_QF", ctx.QF), ("d_KF", ctx.KF),
                                ("d_VF", ctx.VF), ("d_gbuf", ctx.gbuf), ("d_AG", ctx.AG), ("d_xn", ctx.xn),
                                ("d_WTq", ctx.WT["q"]), ("d_WTv", ctx.WT["v"]), ("d_WOT", ctx.WOT)]:
                    dd.inc(ctx.sp.dma_start(out=dbg[nm], in_=tns[:]), 16)
                dd.wait(ctx.sp)
    with nc.Block("fin", no_gpsimd_drain=True) as blk:
        @blk.sync
        def _(sp):
            ctx.sc("ystore").wait(ctx.sp)
    return nc


def emit_l0aa_weights(ctx, wt):
    nc, sp, g = ctx.nc, ctx.sp, ctx.g
    ld = ctx.sc("w_ld")
    ctx.WL = {}
    ctx.WH = {}
    for nm in ("q", "k", "v"):
        ctx.WL[nm] = ctx.alloc(f"WL{nm}", [128, 8 * 256], BF16)
        ctx.WH[nm] = ctx.alloc(f"WH{nm}", [128, 2 * 1024], BF16)
        for kt in range(8):
            ld.inc(g.dma_start(out=ctx.WL[nm][:, kt * 256:(kt + 1) * 256],
                               in_=wt[f"w{nm}_lat"][kt * 128:(kt + 1) * 128, :]), 16)
        for lt in range(2):
            ld.inc(g.dma_start(out=ctx.WH[nm][:, lt * 1024:(lt + 1) * 1024],
                               in_=wt[f"w{nm}_heads"][lt * 128:(lt + 1) * 128, :]), 16)
    ctx.WP = ctx.alloc("WP", [128, 8192], BF16)
    for kt in range(8):
        ld.inc(g.dma_start(out=ctx.WP[:, kt * 1024:(kt + 1) * 1024],
                           in_=wt["w_proj"][kt * 128:(kt + 1) * 128, :]), 16)
    ctx.bias_t = load_bias_cols(ctx, "bias_t", wt["b_proj"])
    ctx.masks = ctx.alloc("masks", [128, 4 * 512], BF16)
    for r in range(4):
        ld.inc(sp.dma_start(out=ctx.masks[:, r * 512:(r + 1) * 512], in_=ctx.cons["cMasks"][r]), 16)
    ctx.ones_bf = ctx.alloc("ones_bf", [128, 64], BF16)
    ctx.ln8n = ctx.alloc("ln8n", [128, 1], F32)
    ctx.QH = ctx.alloc("QH", [128, 2048], BF16)
    ctx.KH = ctx.alloc("KH", [128, 2048], BF16)
    ctx.VH = ctx.alloc("VH", [128, 16 * 65], BF16)
    ctx.xlat = {nm: ctx.alloc(f"xlat{nm}", [128, 2 * 2048], BF16) for nm in ("q", "k", "v")}
    ctx.ATT = ctx.alloc("ATT", [128, NK * T], BF16)
    ctx.s_t = [ctx.alloc(f"s_t{i}", [128, 512], BF16) for i in range(8)]
    ctx.sqx = ctx.alloc("sqx", [128, 512], BF16)
    ctx.q2row = ctx.alloc("q2row", [1, 512], F32)
    ctx.k2row = ctx.alloc("k2row", [1, 2048], F32)
    ctx.smalls = ctx.alloc("smalls", [1, 16], F32)
    ctx.rec_row = ctx.alloc("rec_row", [1, 512], F32)
    ctx.att_s = ctx.alloc("att_s", [128, 512], BF16)
    ctx.rec_bf = ctx.alloc("rec_bf", [1, 512], BF16)
    g.memset(ctx.ones_bf[:], 1.0)
    g.memset(ctx.ln8n[:], float(np.log(0.125)))
    g.memset(ctx.KH[64:65, :], 1.0)
    ld.inc(g.memset(ctx.VH[:].rearrange("p (j c) -> p j c", c=65)[:, :, 64:65], 1.0))
    for e in (ctx.t, ctx.s, ctx.v):
        ld.wait(e)
    ctx.sc("bias_ld").wait(ctx.s)
    ctx.sc("bias_ld").wait(ctx.t)


def emit_l0aa_b(ctx, x_hbm, y_hbm, b):
    nc, t, v, s, sp, g = ctx.nc, ctx.t, ctx.v, ctx.s, ctx.sp, ctx.g
    mm = ctx.sc("aa_mm")
    ev = ctx.sc("aa_ev")
    vv = ctx.sc("aa_vv")
    sc_exp = ctx.sc("aa_exp")
    sc_av = ctx.sc("aa_av")
    sc_msk = ctx.sc("aa_msk")
    if FENCE_L0AA:
        ctx.fence_all()
    vv.wait(t)
    for nm in ("q", "k", "v"):
        for ot2 in range(2):
            for tg in range(4):
                bk = ctx.bank()
                ps = bk.begin(t)
                for kt in range(8):
                    ins = t.matmul(ps[:], ctx.WL[nm][:, kt * 256 + ot2 * 128: kt * 256 + ot2 * 128 + 128],
                                   ctx.xn[:, kt * T + tg * 512: kt * T + tg * 512 + 512],
                                   start=(kt == 0), stop=(kt == 7))
                mm.inc(ins)
                mm.wait(s)
                bk.end(s.activation(ctx.xlat[nm][:, ot2 * 2048 + tg * 512: ot2 * 2048 + tg * 512 + 512],
                                    ps[:], AF.Identity), ev)
    for h in range(16 if L0AA_PARTS >= 1 else 0):
        if FENCE_L0AA:
            ctx.fence_all()
        ev.wait(t)
        sc_av.wait(s)
        sc_av.wait(v)
        for tg in range(4):
            bkk = ctx.bank()
            psk = bkk.begin(t)
            for lt in range(2):
                ins = t.matmul(psk[0:64, :], ctx.WH["k"][:, lt * 1024 + h * 64: lt * 1024 + h * 64 + 64],
                               ctx.xlat["k"][:, lt * 2048 + tg * 512: lt * 2048 + tg * 512 + 512],
                               start=(lt == 0), stop=(lt == 1))
            mm.inc(ins)
            mm.wait(s)
            ev.inc(s.activation(ctx.KH[0:64, tg * 512:(tg + 1) * 512], psk[0:64, :], AF.Identity, scale=0.125))
            ev.inc(s.activation(ctx.sqx[0:64, :], psk[0:64, :], AF.Square))
            bkk.release_at(ev)
            ev.wait(t)
            bk2 = ctx.bank()
            ps2 = bk2.begin(t)
            mm.inc(t.matmul(ps2[0:1, 0:512], ctx.ones_bf[0:64, 0:1], ctx.sqx[0:64, :], start=True, stop=True))
            mm.wait(v)
            bk2.end(v.tensor_copy(ctx.k2row[0:1, tg * 512:(tg + 1) * 512], ps2[0:1, 0:512]), vv)
        vv.wait(v)
        vv.inc(v.tensor_reduce(ctx.smalls[0:1, 0:1], ctx.k2row[:], AX.X, ALU.max))
        vv.wait(s)
        s.activation(ctx.smalls[0:1, 1:2], ctx.smalls[0:1, 0:1], AF.Ln)
        ev.inc(s.activation(ctx.smalls[0:1, 2:3], ctx.smalls[0:1, 1:2], AF.Identity, scale=0.5,
                            bias=ctx.ln8n[0:1, :]))
        for tg in range(4 if L0AA_PARTS >= 2 else 0):
            bkq = ctx.bank()
            psq = bkq.begin(t)
            for lt in range(2):
                ins = t.matmul(psq[0:64, :], ctx.WH["q"][:, lt * 1024 + h * 64: lt * 1024 + h * 64 + 64],
                               ctx.xlat["q"][:, lt * 2048 + tg * 512: lt * 2048 + tg * 512 + 512],
                               start=(lt == 0), stop=(lt == 1))
            mm.inc(ins)
            mm.wait(s)
            ev.inc(s.activation(ctx.QH[0:64, tg * 512:(tg + 1) * 512], psq[0:64, :], AF.Identity))
            ev.inc(s.activation(ctx.sqx[0:64, :], psq[0:64, :], AF.Square))
            bkq.release_at(ev)
            ev.wait(t)
            bq2 = ctx.bank()
            ps2 = bq2.begin(t)
            mm.inc(t.matmul(ps2[0:1, 0:512], ctx.ones_bf[0:64, 0:1], ctx.sqx[0:64, :], start=True, stop=True))
            mm.wait(v)
            vv.wait(v)
            bq2.end(v.tensor_copy(ctx.q2row[:], ps2[0:1, 0:512]), vv)
            vv.wait(s)
            s.activation(ctx.q2row[:], ctx.q2row[:], AF.Ln)
            ev.inc(s.activation(ctx.q2row[:], ctx.q2row[:], AF.Exp, scale=0.5, bias=ctx.smalls[0:1, 2:3]))
            ev.wait(v)
            vv.inc(v.tensor_scalar(ctx.QH[64:65, tg * 512:(tg + 1) * 512], ctx.q2row[:], -1.0, None, ALU.mult))
        for tt in range(NTT):
            bkv = ctx.bank()
            psv = bkv.begin(t)
            ins = None
            for lt in range(2):
                ins = t.matmul(psv[:, 0:64],
                               ctx.xlat["v"][:, lt * 2048 + tt * 128: lt * 2048 + tt * 128 + 128],
                               ctx.WH["v"][:, lt * 1024 + h * 64: lt * 1024 + h * 64 + 64],
                               start=(lt == 0), stop=(lt == 1))
            mm.inc(ins)
            mm.wait(s)
            bkv.end(s.activation(ctx.VH[:, tt * 65: tt * 65 + 64], psv[:, 0:64], AF.Identity), ev)
        ev.wait(t)
        vv.wait(t)
        for ig in range(4):
            njt = 4 * ig + 4
            for jt in range(njt):
                bs = ctx.bank()
                pss = bs.begin(t)
                mm.inc(t.matmul(pss[:], ctx.KH[0:65, jt * 128: jt * 128 + 128],
                                ctx.QH[0:65, ig * 512: ig * 512 + 512], start=True, stop=True))
                stile = ctx.s_t[jt][:] if jt < 8 else ctx.y_tmp[:, (jt - 8) * 512:(jt - 7) * 512]
                mm.wait(s)
                sc_av.wait(s)
                sc_msk.wait(s)
                bs.end(s.activation(stile, pss[0:128, :], AF.Exp), sc_exp)
                r = jt - 4 * ig
                if r >= 0:
                    sc_exp.wait(v)
                    sc_av.wait(v)
                    sc_msk.inc(v.tensor_tensor(stile, stile, ctx.masks[:, r * 512:(r + 1) * 512], ALU.mult))
            sc_exp.wait(t)
            sc_msk.wait(t)
            bo = ctx.hold[0]
            po = bo.begin(t)
            ins = None
            for jt in range(njt):
                stile = ctx.s_t[jt][:] if jt < 8 else ctx.y_tmp[:, (jt - 8) * 512:(jt - 7) * 512]
                ins = t.matmul(po[0:65, :], ctx.VH[:, jt * 65: jt * 65 + 65], stile,
                               start=(jt == 0), stop=(jt == njt - 1))
            sc_av.inc(ins)
            sc_av.wait(v)
            vv.inc(v.reciprocal(ctx.rec_row[:], po[64:65, :]))
            vv.wait(v)
            vv.inc(v.tensor_scalar(ctx.rec_bf[:], ctx.rec_row[:], 1.0, None, ALU.mult))
            vv.wait(s)
            ev.inc(s.activation(ctx.att_s[0:64, :], po[0:64, :], AF.Identity))
            bo.release_at(ev)
            ev.wait(t)
            vv.wait(t)
            br = ctx.bank()
            pr = br.begin(t)
            mm.inc(t.matmul(pr[0:64, :], ctx.ones_bf[0:1, 0:64], ctx.rec_bf[:], start=True, stop=True))
            mm.wait(v)
            ev.wait(v)
            dstp = (h % 2) * 64
            dst = ctx.ATT[dstp:dstp + 64, (h // 2) * 2048 + ig * 512: (h // 2) * 2048 + ig * 512 + 512]
            br.end(v.tensor_tensor(dst, ctx.att_s[0:64, :], pr[0:64, :], ALU.mult), vv)
    if FENCE_L0AA:
        ctx.fence_all()
    vv.wait(t)
    for tg in range(4):
        if FENCE_L0AA:
            ctx.fence_all()
        for ot in range(8):
            bk = ctx.bank()
            ps = bk.begin(t)
            for kt in range(8):
                ins = t.matmul(ps[:], ctx.WP[:, kt * 1024 + ot * 128: kt * 1024 + ot * 128 + 128],
                               ctx.ATT[:, kt * T + tg * 512: kt * T + tg * 512 + 512],
                               start=(kt == 0), stop=(kt == 7))
            mm.inc(ins)
            mm.wait(s)
            bk.end(s.activation(ctx.y_tmp[:, ot * 512:(ot + 1) * 512], ps[:], AF.Identity,
                                bias=ctx.bias_t[:, ot:ot + 1]), ev)
        ev.wait(t)
        ev.wait(v)
        emit_finale_tg(ctx, x_hbm, y_hbm, b, tg, ctx.y_tmp)


def build_l0aa_program(debug=False):
    nc = bass.Bass("TRN2", target_bir_lowering=False, debug=False)
    x_in = nc.dram_tensor("x_in", [NB, T, D], F32, kind="ExternalInput").ap()
    x_out = nc.dram_tensor("x_out", [NB, T, D], F32, kind="ExternalOutput").ap()
    wt = {}
    for nm, shp in [("wq_lat", [1024, 256]), ("wk_lat", [1024, 256]), ("wv_lat", [1024, 256]),
                    ("wq_heads", [256, 1024]), ("wk_heads", [256, 1024]), ("wv_heads", [256, 1024]),
                    ("w_proj", [1024, 1024]), ("b_proj", [1024])]:
        wt[nm] = nc.dram_tensor(nm, shp, F32, kind="ExternalInput").ap()
    ctx = Ctx(nc)
    with nc.Block("init", no_gpsimd_drain=True) as blk:
        @blk.sync
        def _(sp):
            ctx.emit_const_init()
            emit_l0aa_weights(ctx, wt)
    for b in range(NB):
        with nc.Block(f"b{b}", no_gpsimd_drain=True) as blk:
            @blk.sync
            def _(sp, b=b):
                emit_ln(ctx, x_in, b)
                ctx.sc("ln_s").wait(ctx.t)
                ctx.sc("ln_tr").wait(ctx.t)
                emit_l0aa_b(ctx, x_in, x_out, b)
    if debug:
        dbg = {}
        for nm, tns in [("d_lng", ctx.lng), ("d_loff", ctx.loff), ("d_QF", ctx.QF), ("d_KF", ctx.KF),
                        ("d_VF", ctx.VF), ("d_gbuf", ctx.gbuf), ("d_AG", ctx.AG), ("d_xn", ctx.xn),
                        ("d_WTq", ctx.WT["q"]), ("d_WTv", ctx.WT["v"]), ("d_WOT", ctx.WOT)]:
            dbg[nm] = nc.dram_tensor(nm, list(tns.shape), tns.dtype, kind="ExternalOutput").ap()
        with nc.Block("dbg", no_gpsimd_drain=True) as blk:
            @blk.sync
            def _(sp):
                dd = ctx.sc("dbgst")
                for nm, tns in [("d_lng", ctx.lng), ("d_loff", ctx.loff), ("d_QF", ctx.QF), ("d_KF", ctx.KF),
                                ("d_VF", ctx.VF), ("d_gbuf", ctx.gbuf), ("d_AG", ctx.AG), ("d_xn", ctx.xn),
                                ("d_WTq", ctx.WT["q"]), ("d_WTv", ctx.WT["v"]), ("d_WOT", ctx.WOT)]:
                    dd.inc(ctx.sp.dma_start(out=dbg[nm], in_=tns[:]), 16)
                dd.wait(ctx.sp)
    with nc.Block("fin", no_gpsimd_drain=True) as blk:
        @blk.sync
        def _(sp):
            ctx.sc("ystore").wait(ctx.sp)
    return nc


def emit_l0ab_weights(ctx, wt):
    nc, t, v, s, sp, g = ctx.nc, ctx.t, ctx.v, ctx.s, ctx.sp, ctx.g
    ld = ctx.sc("w_ld")
    mmw = ctx.sc("w_mm")
    evw = ctx.sc("w_ev")
    ctx.WT = {}
    for nm in ("q", "k", "v", "g"):
        ctx.WT[nm] = ctx.alloc(f"WT{nm}", [128, 4096], BF16)
    ctx.WOT = ctx.alloc("WOT", [128, 4096], BF16)
    with nc.sbuf_tensor("wstage", [128, 1024], BF16) as wst:
        for nm in ("q", "k", "v", "g"):
            for ot4 in range(4):
                evw.wait(g)
                ld.inc(g.dma_start(out=wst[:], in_=wt["W_" + nm.upper() if nm != "g" else "W_g_w"]
                                   [ot4 * 128:(ot4 + 1) * 128, :]), 16)
                ld.wait(t)
                trb = ctx.trb[ot4 % 2]
                ps = trb.begin(t)
                for kt in range(8):
                    ins = t.transpose(ps[:, (kt % 8) * 128:(kt % 8) * 128 + 128],
                                      wst[:, kt * 128:(kt + 1) * 128], ctx.eye[:])
                mmw.inc(ins)
                mmw.wait(v)
                dst = ctx.WT[nm][:].rearrange("p (k o) -> p k o", o=512)[:, :, ot4 * 128:ot4 * 128 + 128]
                src3 = ps[:].rearrange("p (k c) -> p k c", c=128)
                trb.end(v.tensor_copy(dst, src3), evw)
        for ot in range(8):
            evw.wait(g)
            ld.inc(g.dma_start(out=wst[:, 0:512], in_=wt["W_O"][ot * 128:(ot + 1) * 128, :]), 16)
            ld.wait(t)
            trb = ctx.trb[ot % 2]
            ps = trb.begin(t)
            for kt4 in range(4):
                ins = t.transpose(ps[:, kt4 * 128:kt4 * 128 + 128],
                                  wst[:, kt4 * 128:(kt4 + 1) * 128], ctx.eye[:])
            mmw.inc(ins)
            mmw.wait(v)
            dst = ctx.WOT[:].rearrange("p (k o) -> p k o", o=1024)[:, :, ot * 128:ot * 128 + 128]
            src3 = ps[:].rearrange("p (c4 c) -> p c4 c", c=128)[:, 0:4, :]
            trb.end(v.tensor_copy(dst, src3), evw)
        evw.wait(sp)
        evw.wait(g)
    ctx.bg_t = ctx.alloc("bg_t", [64, 8], F32)
    for hh in range(8):
        srcb = wt["W_g_b"][hh * 64:(hh + 1) * 64].rearrange("(p one) -> p one", one=1)
        ld.inc(sp.dma_start(out=ctx.bg_t[0:64, hh:hh + 1], in_=srcb), 16)
    ctx.masks = ctx.alloc("masks", [128, 4 * 512], BF16)
    for r in range(4):
        ld.inc(sp.dma_start(out=ctx.masks[:, r * 512:(r + 1) * 512], in_=ctx.cons["cMasks"][r]), 16)
    ctx.deltas = ctx.alloc("deltas", [128, 5 * 512], mybir.dt.float16)
    for r in range(5):
        ld.inc(sp.dma_start(out=ctx.deltas[:, r * 512:(r + 1) * 512], in_=ctx.cons["cDeltas"][r]), 16)
    ctx.lng = ctx.alloc("lng", [128, 8], F32)
    ctx.loff = ctx.alloc("loff", [128, 8 * 20], F32)
    ctx.dl_row = ctx.alloc("dl_row", [1, 32], F32)
    ctx.offs = ctx.alloc("offs", [1, 20], F32)
    ctx.loff_row = ctx.alloc("loff_row", [1, 160], F32)
    ld.inc(sp.dma_start(out=ctx.dl_row[0:1, 0:8], in_=wt["decay_log"].rearrange("(a k) -> a k", a=1)), 16)
    ld.inc(sp.dma_start(out=ctx.offs[:], in_=ctx.cons["cOffs"]), 16)
    ld.wait(s)
    ld.wait(v)
    s.activation(ctx.dl_row[0:1, 8:16], ctx.dl_row[0:1, 0:8], AF.Sigmoid)
    evw.inc(s.activation(ctx.dl_row[0:1, 16:24], ctx.dl_row[0:1, 8:16], AF.Ln))
    evw.wait(v)
    hpw = ctx.sc("w_hop")
    for h in range(8):
        hpw.inc(v.tensor_scalar(ctx.loff_row[0:1, h * 20:(h + 1) * 20], ctx.offs[:],
                                ctx.dl_row[0:1, 16 + h:17 + h], None, ALU.mult))
    gg = ctx.sc("w_g")
    bsc = nc.dram_tensor("bscratch", [1, 256], F32).ap()
    hpw.wait(sp)
    evw.wait(sp)
    gg.inc(sp.dma_start(out=bsc[0:1, 0:8], in_=ctx.dl_row[0:1, 16:24]), 16)
    gg.inc(sp.dma_start(out=bsc[0:1, 64:224], in_=ctx.loff_row[:]), 16)
    gg.wait(sp)
    gg.inc(sp.dma_start(out=ctx.lng[:], in_=bsc[0:1, 0:8].partition_broadcast(128).squeeze(1)), 16)
    gg.inc(sp.dma_start(out=ctx.loff[:], in_=bsc[0:1, 64:224].partition_broadcast(128).squeeze(1)), 16)
    ctx.QF = ctx.alloc("QF", [128, 4 * T], BF16)
    ctx.KF = ctx.alloc("KF", [128, 4 * T], BF16)
    ctx.VF = ctx.alloc("VF", [128, 16 * 520], BF16)
    ctx.AG = ctx.alloc("AG", [128, 4 * T], BF16)
    ctx.gbuf = ctx.alloc("gbuf", [128, 2048], BF16)
    ctx.gamb = ctx.alloc("gamb", [128, 512], BF16)
    ctx.s_t = [ctx.alloc(f"s_t{i}", [128, 512], BF16) for i in range(8)]
    ctx.eb1 = ctx.alloc("eb1", [128, 512], F32)
    ctx.eb2 = ctx.alloc("eb2", [128, 512], F32)
    ctx.rec_row = ctx.alloc("rec_row", [1, 512], F32)
    ctx.rec_bf = ctx.alloc("rec_bf", [1, 512], BF16)
    ctx.att_s = ctx.alloc("att_s", [128, 512], BF16)
    ctx.att_g = ctx.alloc("att_g", [128, 512], BF16)
    ctx.ones_bf = ctx.alloc("ones_bf", [128, 64], BF16)
    g.memset(ctx.ones_bf[:], 1.0)
    last_ms = None
    for hh in range(8):
        last_ms = g.memset(ctx.VF[:].rearrange("p (j q) -> p j q", q=520)[:, :, hh * 65 + 64:hh * 65 + 65], 1.0)
    gg.inc(last_ms)
    for e in (ctx.t, ctx.s, ctx.v):
        ld.wait(e); gg.wait(e); evw.wait(e); ctx.sc("bias_ld").wait(e)


def load_bias_cols_n(ctx, name, b_hbm, ncols):
    bt = ctx.alloc(name, [128, ncols], F32)
    ld = ctx.sc("bias_ld")
    for k in range(ncols):
        src = b_hbm[k * 128:(k + 1) * 128].rearrange("(p one) -> p one", one=1)
        ld.inc(ctx.sp.dma_start(out=bt[:, k:k + 1], in_=src), 16)
    return bt


def emit_l0ab_b(ctx, x_hbm, y_hbm, b):
    nc, t, v, s, sp, g = ctx.nc, ctx.t, ctx.v, ctx.s, ctx.sp, ctx.g
    mm = ctx.sc("ab_mm")
    ev = ctx.sc("ab_ev")
    vv = ctx.sc("ab_vv")
    sc_exp = ctx.sc("ab_exp")
    sc_av = ctx.sc("ab_av")
    hp = ctx.sc("ab_hop")
    vv.wait(t)
    for nm, dstT in (("q", ctx.QF), ("k", ctx.KF)):
        for ot4 in range(4):
            for tg in range(4):
                bk = ctx.bank()
                ps = bk.begin(t)
                for kt in range(8):
                    ins = t.matmul(ps[:], ctx.WT[nm][:, kt * 512 + ot4 * 128: kt * 512 + ot4 * 128 + 128],
                                   ctx.xn[:, kt * T + tg * 512: kt * T + tg * 512 + 512],
                                   start=(kt == 0), stop=(kt == 7))
                mm.inc(ins)
                mm.wait(v)
                ev.wait(v)
                vv.wait(v)
                v.tensor_scalar(ctx.eb1[:], ps[:], 0.0, None, ALU.min)
                vv.inc(v.tensor_scalar(ctx.eb2[:], ps[:], 0.0, None, ALU.max))
                bk.release_at(vv)
                vv.wait(s)
                ev.inc(s.activation(ctx.eb1[:], ctx.eb1[:], AF.Exp))
                ev.wait(v)
                vv.inc(v.tensor_tensor(dstT[:, ot4 * T + tg * 512: ot4 * T + tg * 512 + 512],
                                       ctx.eb1[:], ctx.eb2[:], ALU.add))
    for tt in range(NTT):
        bk = ctx.bank()
        ps = bk.begin(t)
        for kt in range(8):
            ins = t.matmul(ps[:], ctx.xn[:, kt * T + tt * 128: kt * T + tt * 128 + 128],
                           ctx.WT["v"][:, kt * 512:(kt + 1) * 512],
                           start=(kt == 0), stop=(kt == 7))
        mm.inc(ins)
        mm.wait(s)
        dst = ctx.VF[:, tt * 520:(tt + 1) * 520].rearrange("p (h c) -> p h c", c=65)[:, :, 0:64]
        src3 = ps[:].rearrange("p (h c) -> p h c", c=64)
        bk.end(s.activation(dst, src3, AF.Identity), ev)
    for h in range(8):
        rows = slice((h % 2) * 64, (h % 2) * 64 + 64)
        cblk = (h // 2) * T
        ev.wait(t)
        sc_av.wait(t)
        vv.wait(s)
        for ig in range(4):
            bk = ctx.bank()
            ps = bk.begin(t)
            for kt in range(8):
                ins = t.matmul(ps[0:64, :], ctx.WT["g"][:, kt * 512 + h * 64: kt * 512 + h * 64 + 64],
                               ctx.xn[:, kt * T + ig * 512: kt * T + ig * 512 + 512],
                               start=(kt == 0), stop=(kt == 7))
            mm.inc(ins)
            mm.wait(s)
            bk.end(s.activation(ctx.gbuf[0:64, ig * 512:(ig + 1) * 512], ps[0:64, :], AF.Sigmoid,
                                bias=ctx.bg_t[0:64, h:h + 1]), ev)
        for ig in range(4):
            njt = 4 * ig + 4
            for jt in range(njt):
                bs = ctx.bank()
                pss = bs.begin(t)
                mm.inc(t.matmul(pss[:], ctx.KF[rows, cblk + jt * 128: cblk + jt * 128 + 128],
                                ctx.QF[rows, cblk + ig * 512: cblk + ig * 512 + 512],
                                start=True, stop=True))
                r = jt - 4 * ig
                ev.wait(s)
                vv.wait(s)
                sc_exp.wait(s)
                if r >= 0:
                    ev.inc(s.activation(ctx.gamb[:], ctx.deltas[:, (r + 1) * 512:(r + 2) * 512],
                                        AF.Exp, scale=ctx.lng[:, h:h + 1]))
                else:
                    off_i = 4 * ig - jt + 3
                    ev.inc(s.activation(ctx.gamb[:], ctx.deltas[:, 0:512], AF.Exp,
                                        scale=ctx.lng[:, h:h + 1],
                                        bias=ctx.loff[:, h * 20 + off_i:h * 20 + off_i + 1]))
                stile = ctx.s_t[jt][:] if jt < 8 else ctx.y_tmp[:, (jt - 8) * 512:(jt - 7) * 512]
                mm.wait(v)
                ev.wait(v)
                sc_av.wait(v)
                ins = v.scalar_tensor_tensor(stile, pss[:], 1.0, ctx.gamb[:], ALU.mult, ALU.mult)
                if r >= 0:
                    hp.inc(ins)
                    hp.wait(v)
                    ins = v.tensor_tensor(stile, stile, ctx.masks[:, r * 512:(r + 1) * 512], ALU.mult)
                bs.end(ins, sc_exp)
            sc_exp.wait(t)
            bo = ctx.hold[0]
            po = bo.begin(t)
            ins = None
            for jt in range(njt):
                stile = ctx.s_t[jt][:] if jt < 8 else ctx.y_tmp[:, (jt - 8) * 512:(jt - 7) * 512]
                ins = t.matmul(po[0:65, :], ctx.VF[:, jt * 520 + h * 65: jt * 520 + h * 65 + 65],
                               stile, start=(jt == 0), stop=(jt == njt - 1))
            sc_av.inc(ins)
            sc_av.wait(v)
            vv.inc(v.tensor_scalar(ctx.rec_row[:], po[64:65, :], 1e-6, None, ALU.add))
            vv.wait(v)
            vv.inc(v.reciprocal(ctx.rec_row[:], ctx.rec_row[:]))
            vv.wait(v)
            vv.inc(v.tensor_scalar(ctx.rec_bf[:], ctx.rec_row[:], 1.0, None, ALU.mult))
            vv.wait(s)
            ev.inc(s.activation(ctx.att_s[0:64, :], po[0:64, :], AF.Identity))
            bo.release_at(ev)
            ev.wait(t)
            vv.wait(t)
            br = ctx.bank()
            pr = br.begin(t)
            mm.inc(t.matmul(pr[0:64, :], ctx.ones_bf[0:1, 0:64], ctx.rec_bf[:], start=True, stop=True))
            mm.wait(v)
            ev.wait(v)
            hp.inc(v.tensor_tensor(ctx.att_g[0:64, :], ctx.att_s[0:64, :], pr[0:64, :], ALU.mult))
            br.release_at(hp)
            hp.wait(v)
            dst = ctx.AG[rows, cblk + ig * 512: cblk + ig * 512 + 512]
            vv.inc(v.tensor_tensor(dst, ctx.att_g[0:64, :], ctx.gbuf[0:64, ig * 512:(ig + 1) * 512], ALU.mult))
    vv.wait(t)
    for tg in range(4):
        for ot in range(8):
            bk = ctx.bank()
            ps = bk.begin(t)
            for kt4 in range(4):
                ins = t.matmul(ps[:], ctx.WOT[:, kt4 * 1024 + ot * 128: kt4 * 1024 + ot * 128 + 128],
                               ctx.AG[:, kt4 * T + tg * 512: kt4 * T + tg * 512 + 512],
                               start=(kt4 == 0), stop=(kt4 == 3))
            mm.inc(ins)
            mm.wait(s)
            bk.end(s.activation(ctx.y_tmp[:, ot * 512:(ot + 1) * 512], ps[:], AF.Identity), ev)
        ev.wait(t)
        ev.wait(v)
        emit_finale_tg(ctx, x_hbm, y_hbm, b, tg, ctx.y_tmp)


def build_l0ab_program(debug=False):
    nc = bass.Bass("TRN2", target_bir_lowering=False, debug=False)
    x_in = nc.dram_tensor("x_in", [NB, T, D], F32, kind="ExternalInput").ap()
    x_out = nc.dram_tensor("x_out", [NB, T, D], F32, kind="ExternalOutput").ap()
    wt = {}
    for nm, shp in [("W_Q", [512, 1024]), ("W_K", [512, 1024]), ("W_V", [512, 1024]),
                    ("W_g_w", [512, 1024]), ("W_g_b", [512]), ("W_O", [1024, 512]),
                    ("decay_log", [8])]:
        wt[nm] = nc.dram_tensor(nm, shp, F32, kind="ExternalInput").ap()
    ctx = Ctx(nc)
    with nc.Block("init", no_gpsimd_drain=True) as blk:
        @blk.sync
        def _(sp):
            ctx.emit_const_init()
            emit_l0ab_weights(ctx, wt)
    for b in range(NB):
        with nc.Block(f"b{b}", no_gpsimd_drain=True) as blk:
            @blk.sync
            def _(sp, b=b):
                emit_ln(ctx, x_in, b)
                ctx.sc("ln_s").wait(ctx.t)
                ctx.sc("ln_tr").wait(ctx.t)
                emit_l0ab_b(ctx, x_in, x_out, b)
    if debug:
        dbg = {}
        for nm, tns in [("d_lng", ctx.lng), ("d_loff", ctx.loff), ("d_QF", ctx.QF), ("d_KF", ctx.KF),
                        ("d_VF", ctx.VF), ("d_gbuf", ctx.gbuf), ("d_AG", ctx.AG), ("d_xn", ctx.xn),
                        ("d_WTq", ctx.WT["q"]), ("d_WTv", ctx.WT["v"]), ("d_WOT", ctx.WOT)]:
            dbg[nm] = nc.dram_tensor(nm, list(tns.shape), tns.dtype, kind="ExternalOutput").ap()
        with nc.Block("dbg", no_gpsimd_drain=True) as blk:
            @blk.sync
            def _(sp):
                dd = ctx.sc("dbgst")
                for nm, tns in [("d_lng", ctx.lng), ("d_loff", ctx.loff), ("d_QF", ctx.QF), ("d_KF", ctx.KF),
                                ("d_VF", ctx.VF), ("d_gbuf", ctx.gbuf), ("d_AG", ctx.AG), ("d_xn", ctx.xn),
                                ("d_WTq", ctx.WT["q"]), ("d_WTv", ctx.WT["v"]), ("d_WOT", ctx.WOT)]:
                    dd.inc(ctx.sp.dma_start(out=dbg[nm], in_=tns[:]), 16)
                dd.wait(ctx.sp)
    with nc.Block("fin", no_gpsimd_drain=True) as blk:
        @blk.sync
        def _(sp):
            ctx.sc("ystore").wait(ctx.sp)
    return nc


def emit_l1ab_weights(ctx, wt):
    nc, t, v, s, sp, g = ctx.nc, ctx.t, ctx.v, ctx.s, ctx.sp, ctx.g
    ld = ctx.sc("w_ld")
    ctx.WOUT = ctx.alloc("WOUT", [128, 8192], BF16)
    for kt in range(8):
        ld.inc(g.dma_start(out=ctx.WOUT[:, kt * 1024:(kt + 1) * 1024],
                           in_=wt["w_out"][kt * 128:(kt + 1) * 128, :]), 16)
    ctx.C128 = ctx.alloc("C128s", [128, 128], BF16)
    ctx.S128 = ctx.alloc("S128s", [128, 128], BF16)
    ctx.S128n = ctx.alloc("S128ns", [128, 128], BF16)
    ld.inc(sp.dma_start(out=ctx.C128[:], in_=ctx.cons["cC128"]), 16)
    ld.inc(sp.dma_start(out=ctx.S128[:], in_=ctx.cons["cS128"]), 16)
    ld.inc(sp.dma_start(out=ctx.S128n[:], in_=ctx.cons["cS128n"]), 16)
    ctx.whd = ctx.alloc("whd", [128, 3 * 1024], BF16)
    ctx.pl = [ctx.alloc(f"pl{i}", [128, T], F32) for i in range(7)]
    ctx.plG = ctx.pl[6]
    ctx.ones2k = ctx.alloc("ones2k", [128, T], F32)
    ld.inc(g.memset(ctx.ones2k[:], 1.0))
    ctx.wrb = ctx.alloc("wrb", [128, T], BF16)
    ctx.wib = ctx.alloc("wib", [128, T], BF16)
    ctx.stg = ctx.wrb
    ctx.VALS = ctx.alloc("VALS", [128, NK * T], BF16)
    ctx.y_tmp32 = ctx.alloc("y32", [128, 4096], F32)
    for e in (ctx.t, ctx.s, ctx.v):
        ld.wait(e)


def emit_l1ab_b(ctx, x_hbm, y_hbm, b, wqkv_hbm):
    nc, t, v, s, sp, g = ctx.nc, ctx.t, ctx.v, ctx.s, ctx.sp, ctx.g
    mm = ctx.sc("abf_mm")
    ev = ctx.sc("abf_ev")
    vv = ctx.sc("abf_vv")
    hp = ctx.sc("abf_hp")
    wld = ctx.sc("abf_wld")
    A, B, C, Dp, E, F, G = ctx.pl

    def vstep(dst, in0, in1, op):
        hp.inc(v.tensor_tensor(dst, in0, in1, op))
        hp.wait(v)

    for h in range(8):
        mm.wait(g)
        for i3, c0 in enumerate((h * 128, 1024 + h * 128, 2048 + h * 128)):
            for kt in range(8):
                wld.inc(g.dma_start(out=ctx.whd[:, i3 * 1024 + kt * 128: i3 * 1024 + kt * 128 + 128],
                                    in_=wqkv_hbm[kt * 128:(kt + 1) * 128, c0:c0 + 128]), 16)
        for i3, (pr_, pi_) in enumerate(((A, B), (C, Dp), (E, F))):
            for tg in range(4):
                bk = ctx.bank()
                ps = bk.begin(t)
                wld.wait(t)
                for kt in range(8):
                    ins = t.matmul(ps[:], ctx.whd[:, i3 * 1024 + kt * 128: i3 * 1024 + kt * 128 + 128],
                                   ctx.xn[:, kt * T + tg * 512: kt * T + tg * 512 + 512],
                                   start=(kt == 0), stop=(kt == 7))
                mm.inc(ins)
                mm.wait(s)
                bk.end(s.activation(ctx.stg[:, tg * 512:(tg + 1) * 512], ps[:], AF.Identity), ev)
                ev.wait(t)
                for (mat, dstp) in ((ctx.C128, pr_), (ctx.S128, pi_)):
                    bk2 = ctx.bank()
                    ps2 = bk2.begin(t)
                    mm.inc(t.matmul(ps2[:], mat[:], ctx.stg[:, tg * 512:(tg + 1) * 512],
                                    start=True, stop=True))
                    mm.wait(v)
                    hp.wait(v)
                    bk2.end(v.tensor_copy(dstp[:, tg * 512:(tg + 1) * 512], ps2[:]), vv)
        vv.wait(v)
        mm.wait(v)
        vstep(G[:], C[:], E[:], ALU.mult)
        vstep(E[:], Dp[:], E[:], ALU.mult)
        vstep(C[:], C[:], F[:], ALU.mult)
        vstep(F[:], Dp[:], F[:], ALU.mult)
        vstep(G[:], G[:], F[:], ALU.subtract)
        vstep(C[:], C[:], E[:], ALU.add)
        hp.inc(v.tensor_tensor_scan(Dp[:], ctx.ones2k[:], G[:], 0.0, ALU.mult, ALU.add))
        hp.wait(v)
        hp.inc(v.tensor_tensor_scan(E[:], ctx.ones2k[:], C[:], 0.0, ALU.mult, ALU.add))
        hp.wait(v)
        vstep(F[:], Dp[:], A[:], ALU.mult)
        vstep(G[:], E[:], B[:], ALU.mult)
        vstep(ctx.wrb[:], F[:], G[:], ALU.add)
        vstep(F[:], Dp[:], B[:], ALU.mult)
        vstep(G[:], E[:], A[:], ALU.mult)
        vstep(ctx.wib[:], F[:], G[:], ALU.subtract)
        hp.wait(t)
        for tg in range(4):
            bk = ctx.bank()
            ps = bk.begin(t)
            t.matmul(ps[:], ctx.C128[:], ctx.wrb[:, tg * 512:(tg + 1) * 512], start=True, stop=False)
            mm.inc(t.matmul(ps[:], ctx.S128n[:], ctx.wib[:, tg * 512:(tg + 1) * 512], start=False, stop=True))
            mm.wait(s)
            bk.end(s.activation(ctx.VALS[:, h * T + tg * 512: h * T + tg * 512 + 512],
                                ps[:], AF.Identity, scale=1.0 / 128.0), ev)
    ev.wait(t)
    for tg in range(4):
        for ot in range(8):
            bk = ctx.bank()
            ps = bk.begin(t)
            for kt in range(8):
                ins = t.matmul(ps[:], ctx.WOUT[:, kt * 1024 + ot * 128: kt * 1024 + ot * 128 + 128],
                               ctx.VALS[:, kt * T + tg * 512: kt * T + tg * 512 + 512],
                               start=(kt == 0), stop=(kt == 7))
            mm.inc(ins)
            mm.wait(s)
            bk.end(s.activation(ctx.y_tmp32[:, ot * 512:(ot + 1) * 512], ps[:], AF.Identity), ev)
        ev.wait(t)
        ev.wait(v)
        emit_finale_tg32(ctx, x_hbm, y_hbm, b, tg, ctx.y_tmp32)


def build_l1ab_program(debug=False):
    nc = bass.Bass("TRN2", target_bir_lowering=False, debug=False)
    x_in = nc.dram_tensor("x_in", [NB, T, D], F32, kind="ExternalInput").ap()
    x_out = nc.dram_tensor("x_out", [NB, T, D], F32, kind="ExternalOutput").ap()
    wqkv = nc.dram_tensor("w_qkv", [1024, 3072], F32, kind="ExternalInput").ap()
    wout = nc.dram_tensor("w_out", [1024, 1024], F32, kind="ExternalInput").ap()
    ctx = Ctx(nc)
    with nc.Block("init", no_gpsimd_drain=True) as blk:
        @blk.sync
        def _(sp):
            ctx.emit_const_init()
            emit_l1ab_weights(ctx, {"w_out": wout})
    for b in range(NB):
        with nc.Block(f"b{b}", no_gpsimd_drain=True) as blk:
            @blk.sync
            def _(sp, b=b):
                emit_ln(ctx, x_in, b)
                ctx.sc("ln_s").wait(ctx.t)
                ctx.sc("ln_tr").wait(ctx.t)
                emit_l1ab_b(ctx, x_in, x_out, b, wqkv)
    with nc.Block("fin", no_gpsimd_drain=True) as blk:
        @blk.sync
        def _(sp):
            ctx.sc("ystore").wait(ctx.sp)
    return nc


def emit_l1aa_weights(ctx, wt):
    nc, t, v, s, sp, g = ctx.nc, ctx.t, ctx.v, ctx.s, ctx.sp, ctx.g
    ld = ctx.sc("w_ld")
    ctx.WO = ctx.alloc("WOsb", [128, 8192], BF16)
    for kt in range(8):
        ld.inc(g.dma_start(out=ctx.WO[:, kt * 1024:(kt + 1) * 1024],
                           in_=wt["WO"][kt * 128:(kt + 1) * 128, :]), 16)
    ctx.bias_t = load_bias_cols(ctx, "bias_t", wt["bO"])
    ctx.bq_t = ctx.alloc("bq_t", [64, 16], F32)
    ctx.bk_t = ctx.alloc("bk_t", [64, 16], F32)
    ctx.bv_t = ctx.alloc("bv_t", [64, 16], F32)
    for bt, nm in ((ctx.bq_t, "bQ"), (ctx.bk_t, "bK"), (ctx.bv_t, "bV")):
        for hh in range(16):
            srcb = wt[nm][hh * 64:(hh + 1) * 64].rearrange("(p one) -> p one", one=1)
            ld.inc(sp.dma_start(out=bt[0:64, hh:hh + 1], in_=srcb), 16)
    ctx.whd = ctx.alloc("whd", [128, 3 * 512], BF16)
    ctx.SST = ctx.alloc("SST", [128, 16 * T], BF16)
    ctx.QH = ctx.alloc("QH", [128, 2048], BF16)
    ctx.KH = ctx.alloc("KH", [128, 2048], BF16)
    ctx.VH = ctx.alloc("VH", [128, 16 * 65], BF16)
    ctx.ATT = ctx.alloc("ATT", [128, NK * T], BF16)
    ctx.NCall = ctx.alloc("NCall", [128, 16], F32)
    ctx.ncacc = ctx.alloc("ncacc", [128, 4], F32)
    ctx.Crow = ctx.alloc("Crow", [128, 16], F32)
    ctx.w1 = ctx.alloc("w1", [128, 512], F32)
    ctx.w2 = ctx.alloc("w2", [128, 512], F32)
    ctx.sqx = ctx.alloc("sqx", [128, 512], BF16)
    ctx.q2row = ctx.alloc("q2row", [1, 512], F32)
    ctx.k2bf = ctx.alloc("k2bf", [1, 2048], BF16)
    ctx.onesrow = ctx.alloc("onesrow", [1, 2048], BF16)
    ctx.rec_row = ctx.alloc("rec_row", [1, 512], F32)
    ctx.rec_bf = ctx.alloc("rec_bf", [1, 512], BF16)
    ctx.att_s = ctx.alloc("att_s", [128, 512], BF16)
    ctx.att_d = ctx.alloc("att_d", [128, 512], BF16)
    ctx.ones_bf = ctx.alloc("ones_bf", [128, 64], BF16)
    g.memset(ctx.ones_bf[:], 1.0)
    g.memset(ctx.onesrow[:], 1.0)
    g.memset(ctx.KH[64:65, :], 1.0)
    ld.inc(g.memset(ctx.sqx[:], 0.0))
    gi = ctx.sc("w_g2")
    gi.wait(sp)
    gi.inc(sp.dma_start(out=ctx.QH[65:66, :], in_=ctx.onesrow[:]), 16)
    for e in (ctx.t, ctx.s, ctx.v):
        ld.wait(e); gi.wait(e); ctx.sc("bias_ld").wait(e)


def emit_l1aa_b(ctx, x_hbm, y_hbm, b, wq_hbm, wk_hbm, wv_hbm):
    nc, t, v, s, sp, g = ctx.nc, ctx.t, ctx.v, ctx.s, ctx.sp, ctx.g
    mm = ctx.sc("a1_mm")
    ev = ctx.sc("a1_ev")
    vv = ctx.sc("a1_vv")
    sc_s = ctx.sc("a1_s")
    sc_av = ctx.sc("a1_av")
    wld = ctx.sc("a1_wld")
    k2d = ctx.sc("a1_k2d")
    for h in range(16):
        mm.wait(g)
        for i3, w_hbm in enumerate((wq_hbm, wk_hbm, wv_hbm)):
            for kt in range(8):
                wld.inc(g.dma_start(out=ctx.whd[:, i3 * 512 + kt * 64: i3 * 512 + kt * 64 + 64],
                                    in_=w_hbm[kt * 128:(kt + 1) * 128, h * 64:(h + 1) * 64]), 16)
        sc_av.wait(s)
        sc_av.wait(v)
        for i3, nm in ((0, "q"), (1, "k")):
            dstT = ctx.QH if nm == "q" else ctx.KH
            bt = ctx.bq_t if nm == "q" else ctx.bk_t
            for tg in range(4):
                bk = ctx.bank()
                ps = bk.begin(t)
                wld.wait(t)
                for kt in range(8):
                    ins = t.matmul(ps[0:64, :], ctx.whd[:, i3 * 512 + kt * 64: i3 * 512 + kt * 64 + 64],
                                   ctx.xn[:, kt * T + tg * 512: kt * T + tg * 512 + 512],
                                   start=(kt == 0), stop=(kt == 7))
                mm.inc(ins)
                mm.wait(s)
                if nm == "q":
                    s.activation(dstT[0:64, tg * 512:(tg + 1) * 512], ps[0:64, :], AF.Identity,
                                 bias=bt[0:64, h:h + 1])
                else:
                    s.activation(ctx.w1[0:64, :], ps[0:64, :], AF.Identity, bias=bt[0:64, h:h + 1])
                    s.activation(dstT[0:64, tg * 512:(tg + 1) * 512], ctx.w1[0:64, :], AF.Identity,
                                 scale=-2.0)
                ev.wait(v)
                bk.release_at(ev)
                ev.inc(s.activation(ctx.sqx[0:64, :], dstT[0:64, tg * 512:(tg + 1) * 512], AF.Square))
                ev.wait(t)
                bq2 = ctx.bank()
                ps2 = bq2.begin(t)
                mm.inc(t.matmul(ps2[0:1, 0:512], ctx.ones_bf[0:64, 0:1], ctx.sqx[0:64, :],
                                start=True, stop=True))
                mm.wait(v)
                vv.wait(v)
                if nm == "q":
                    bq2.end(v.tensor_copy(ctx.QH[64:65, tg * 512:(tg + 1) * 512], ps2[0:1, 0:512]), vv)
                else:
                    bq2.end(v.tensor_scalar(ctx.k2bf[0:1, tg * 512:(tg + 1) * 512], ps2[0:1, 0:512],
                                            0.25, None, ALU.mult), vv)
        vv.wait(sp)
        ev.wait(sp)
        k2d.inc(sp.dma_start(out=ctx.KH[65:66, :], in_=ctx.k2bf[:]), 16)
        k2d.wait(t)
        ev.wait(t)
        vv.wait(t)
        for jt in range(16):
            for ig in range(4):
                bs = ctx.bank()
                pss = bs.begin(t)
                sc_s.wait(t)
                mm.inc(t.matmul(pss[:], ctx.KH[0:66, jt * 128: jt * 128 + 128],
                                ctx.QH[0:66, ig * 512: ig * 512 + 512], start=True, stop=True))
                mm.wait(s)
                vv.wait(s)
                s.activation(ctx.w1[:], pss[:], AF.Ln, scale=1.0 / 4096.0)
                bs.release_at(ctx._sc["a1_ev"]) if False else None
                bs.end(s.activation(ctx.w2[:], ctx.w1[:], AF.Exp, scale=0.5), ev)
                s.activation(ctx.w1[:], ctx.w2[:], AF.Ln, bias=ctx.onesf[:])
                sc_s.inc(s.activation(ctx.SST[:, jt * T + ig * 512: jt * T + ig * 512 + 512],
                                      ctx.w1[:], AF.Exp, scale=-65.5,
                                      accum_out=ctx.ncacc[:, ig:ig + 1]))
            sc_s.inc(s.add(ctx.stat[:, 0:1], ctx.ncacc[:, 3:4], 0.0))
            sc_s.wait(v)
            vv.inc(v.tensor_reduce(ctx.NCall[:, jt:jt + 1], ctx.ncacc[:], AX.X, ALU.add))
        vv.wait(s)
        s.activation(ctx.Crow[:], ctx.NCall[:], AF.Ln)
        ev.inc(s.activation(ctx.Crow[:], ctx.Crow[:], AF.Exp, scale=-0.5))
        ev.wait(t)
        ev.wait(v)
        for half in range(2):
            bkv = ctx.bank()
            psv = bkv.begin(t)
            for j8 in range(8):
                tt = half * 8 + j8
                for kt in range(8):
                    ins = t.matmul(psv[:, j8 * 64:(j8 + 1) * 64],
                                   ctx.xn[:, kt * T + tt * 128: kt * T + tt * 128 + 128],
                                   ctx.whd[:, 1024 + kt * 64: 1024 + kt * 64 + 64],
                                   start=(kt == 0), stop=(kt == 7))
            mm.inc(ins)
            mm.wait(v)
            last = None
            for j8 in range(8):
                tt = half * 8 + j8
                v.tensor_scalar(ctx.VH[:, tt * 65: tt * 65 + 64], psv[:, j8 * 64:(j8 + 1) * 64],
                                ctx.Crow[:, tt:tt + 1], None, ALU.mult)
                last = v.tensor_scalar(ctx.VH[:, tt * 65 + 64: tt * 65 + 65], ctx.Crow[:, tt:tt + 1],
                                       1.0, None, ALU.mult)
            bkv.end(last, vv)
        vv.wait(t)
        sc_s.wait(t)
        for ig in range(4):
            bo = ctx.hold[0]
            po = bo.begin(t)
            for jt in range(16):
                sc_av.inc(t.matmul(po[0:65, :], ctx.VH[:, jt * 65: jt * 65 + 65],
                                   ctx.SST[:, jt * T + ig * 512: jt * T + ig * 512 + 512],
                                   start=(jt == 0), stop=(jt == 15)))
            sc_av.wait(v)
            vv.wait(v)
            vv.inc(v.reciprocal(ctx.rec_row[:], po[64:65, :]))
            vv.wait(v)
            vv.inc(v.tensor_scalar(ctx.rec_bf[:], ctx.rec_row[:], 1.0, None, ALU.mult))
            vv.wait(s)
            ev.inc(s.activation(ctx.att_s[0:64, :], po[0:64, :], AF.Identity))
            bo.release_at(ev)
            ev.wait(t)
            vv.wait(t)
            br = ctx.bank()
            pr = br.begin(t)
            mm.inc(t.matmul(pr[0:64, :], ctx.ones_bf[0:1, 0:64], ctx.rec_bf[:], start=True, stop=True))
            mm.wait(v)
            ev.wait(v)
            br.end(v.tensor_tensor(ctx.att_d[0:64, :], ctx.att_s[0:64, :], pr[0:64, :], ALU.mult), vv)
            vv.wait(s)
            dstp = (h % 2) * 64
            dst = ctx.ATT[dstp:dstp + 64, (h // 2) * T + ig * 512: (h // 2) * T + ig * 512 + 512]
            ev.inc(s.activation(dst, ctx.att_d[0:64, :], AF.Identity, bias=ctx.bv_t[0:64, h:h + 1]))
    ev.wait(t)
    for tg in range(4):
        for ot in range(8):
            bk = ctx.bank()
            ps = bk.begin(t)
            for kt in range(8):
                ins = t.matmul(ps[:], ctx.WO[:, kt * 1024 + ot * 128: kt * 1024 + ot * 128 + 128],
                               ctx.ATT[:, kt * T + tg * 512: kt * T + tg * 512 + 512],
                               start=(kt == 0), stop=(kt == 7))
            mm.inc(ins)
            mm.wait(s)
            bk.end(s.activation(ctx.y_tmp[:, ot * 512:(ot + 1) * 512], ps[:], AF.Identity,
                                bias=ctx.bias_t[:, ot:ot + 1]), ev)
        ev.wait(t)
        ev.wait(v)
        emit_finale_tg(ctx, x_hbm, y_hbm, b, tg, ctx.y_tmp)


def build_l1aa_program(debug=False):
    nc = bass.Bass("TRN2", target_bir_lowering=False, debug=False)
    x_in = nc.dram_tensor("x_in", [NB, T, D], F32, kind="ExternalInput").ap()
    x_out = nc.dram_tensor("x_out", [NB, T, D], F32, kind="ExternalOutput").ap()
    wt = {}
    for nm in ("WQ", "WK", "WV", "WO"):
        wt[nm] = nc.dram_tensor(nm, [1024, 1024], F32, kind="ExternalInput").ap()
    for nm in ("bQ", "bK", "bV", "bO"):
        wt[nm] = nc.dram_tensor(nm, [1024], F32, kind="ExternalInput").ap()
    ctx = Ctx(nc)
    with nc.Block("init", no_gpsimd_drain=True) as blk:
        @blk.sync
        def _(sp):
            ctx.emit_const_init()
            emit_l1aa_weights(ctx, wt)
    for b in range(NB):
        with nc.Block(f"b{b}", no_gpsimd_drain=True) as blk:
            @blk.sync
            def _(sp, b=b):
                emit_ln(ctx, x_in, b)
                ctx.sc("ln_s").wait(ctx.t)
                ctx.sc("ln_tr").wait(ctx.t)
                emit_l1aa_b(ctx, x_in, x_out, b, wt["WQ"], wt["WK"], wt["WV"])
    with nc.Block("fin", no_gpsimd_drain=True) as blk:
        @blk.sync
        def _(sp):
            ctx.sc("ystore").wait(ctx.sp)
    return nc


def build_fused_program(nphases=8):
    nc = bass.Bass("TRN2", target_bir_lowering=False, debug=False)
    x_in = nc.dram_tensor("x_in", [NB, T, D], F32, kind="ExternalInput").ap()
    x_out = nc.dram_tensor("x_out", [NB, T, D], F32, kind="ExternalOutput").ap()
    xs_a = nc.dram_tensor("xs_a", [NB, T, D], F32).ap()
    xs_b = nc.dram_tensor("xs_b", [NB, T, D], F32).ap()
    A_scr = nc.dram_tensor("A_scr", [128, 8192], BF16).ap()
    cT_scr = nc.dram_tensor("cT_scr", [128, 8192], BF16).ap()
    sP_scr = nc.dram_tensor("sP_scr", [128, 8192], BF16).ap()

    wt = {}
    for nm, shp in [("L0_attn_a_wq_lat", [1024, 256]), ("L0_attn_a_wk_lat", [1024, 256]),
                    ("L0_attn_a_wv_lat", [1024, 256]), ("L0_attn_a_wq_heads", [256, 1024]),
                    ("L0_attn_a_wk_heads", [256, 1024]), ("L0_attn_a_wv_heads", [256, 1024]),
                    ("L0_attn_a_w_proj", [1024, 1024]), ("L0_attn_a_b_proj", [1024]),
                    ("L0_attn_b_W_Q", [512, 1024]), ("L0_attn_b_W_K", [512, 1024]),
                    ("L0_attn_b_W_V", [512, 1024]), ("L0_attn_b_W_g_w", [512, 1024]),
                    ("L0_attn_b_W_g_b", [512]), ("L0_attn_b_W_O", [1024, 512]),
                    ("L0_attn_b_decay_log", [8]),
                    ("L0_ffn_w_freq_real", [1024, 1024]), ("L0_ffn_w_freq_imag", [1024, 1024]),
                    ("L0_ffn_bias", [1024]),
                    ("L1_attn_a_WQ", [1024, 1024]), ("L1_attn_a_bQ", [1024]),
                    ("L1_attn_a_WK", [1024, 1024]), ("L1_attn_a_bK", [1024]),
                    ("L1_attn_a_WV", [1024, 1024]), ("L1_attn_a_bV", [1024]),
                    ("L1_attn_a_WO", [1024, 1024]), ("L1_attn_a_bO", [1024]),
                    ("L1_attn_b_w_qkv", [1024, 3072]), ("L1_attn_b_w_out", [1024, 1024]),
                    ("L1_ffn_theta", [1024, 1024]), ("L1_ffn_phi", [1024, 1024]),
                    ("L1_ffn_theta_noise", [1024, 1024]), ("L1_ffn_phi_noise", [1024, 1024])]:
        wt[nm] = nc.dram_tensor(nm, shp, F32, kind="ExternalInput").ap()

    ctx = Ctx(nc)
    with nc.Block("init", no_gpsimd_drain=True) as blk:
        @blk.sync
        def _(sp):
            ctx.emit_const_init()

    stream = [x_in]
    for i in range(nphases - 1):
        stream.append(xs_a if i % 2 == 0 else xs_b)
    stream.append(x_out)

    def ph_l0aa(b, xi, xo):
        emit_l0aa_b(ctx, xi, xo, b)

    def ph_l0ffn(b, xi, xo):
        emit_l0ffn_b(ctx, xi, xo, b, ctx.bias_t)

    def ph_l0ab(b, xi, xo):
        emit_l0ab_b(ctx, xi, xo, b)

    def ph_l1aa(b, xi, xo):
        emit_l1aa_b(ctx, xi, xo, b, wt["L1_attn_a_WQ"], wt["L1_attn_a_WK"], wt["L1_attn_a_WV"])

    def ph_l1ffn(b, xi, xo):
        emit_l1ffn_b(ctx, xi, xo, b)

    def ph_l1ab(b, xi, xo):
        emit_l1ab_b(ctx, xi, xo, b, wt["L1_attn_b_w_qkv"])

    def w_l0aa():
        emit_l0aa_weights(ctx, {
            "wq_lat": wt["L0_attn_a_wq_lat"], "wk_lat": wt["L0_attn_a_wk_lat"],
            "wv_lat": wt["L0_attn_a_wv_lat"], "wq_heads": wt["L0_attn_a_wq_heads"],
            "wk_heads": wt["L0_attn_a_wk_heads"], "wv_heads": wt["L0_attn_a_wv_heads"],
            "w_proj": wt["L0_attn_a_w_proj"], "b_proj": wt["L0_attn_a_b_proj"]})

    def w_l0ffn(mode):
        def f():
            emit_build_A(ctx, wt["L0_ffn_w_freq_real"], wt["L0_ffn_w_freq_imag"],
                         cache=A_scr, mode=mode)
            ctx.bias_t = load_bias_cols(ctx, "bias_t", wt["L0_ffn_bias"])
            ctx.sc("bias_ld").wait(ctx.s)
        return f

    def w_l0ab():
        emit_l0ab_weights(ctx, {
            "W_Q": wt["L0_attn_b_W_Q"], "W_K": wt["L0_attn_b_W_K"], "W_V": wt["L0_attn_b_W_V"],
            "W_g_w": wt["L0_attn_b_W_g_w"], "W_g_b": wt["L0_attn_b_W_g_b"],
            "W_O": wt["L0_attn_b_W_O"], "decay_log": wt["L0_attn_b_decay_log"]})

    def w_l1aa():
        emit_l1aa_weights(ctx, {
            "WQ": wt["L1_attn_a_WQ"], "WK": wt["L1_attn_a_WK"], "WV": wt["L1_attn_a_WV"],
            "WO": wt["L1_attn_a_WO"], "bQ": wt["L1_attn_a_bQ"], "bK": wt["L1_attn_a_bK"],
            "bV": wt["L1_attn_a_bV"], "bO": wt["L1_attn_a_bO"]})

    def w_l1ffn(mode):
        def f():
            emit_l1ffn_weights(ctx, {"theta": wt["L1_ffn_theta"], "phi": wt["L1_ffn_phi"],
                                     "theta_noise": wt["L1_ffn_theta_noise"],
                                     "phi_noise": wt["L1_ffn_phi_noise"]},
                               cache=(cT_scr, sP_scr), mode=mode)
        return f

    def w_l1ab():
        emit_l1ab_weights(ctx, {"w_out": wt["L1_attn_b_w_out"]})

    plan = [("q0", w_l0aa, ph_l0aa), ("q1", w_l0ffn("build"), ph_l0ffn),
            ("q2", w_l0ab, ph_l0ab), ("q3", w_l0ffn("load"), ph_l0ffn),
            ("q4", w_l1aa, ph_l1aa), ("q5", w_l1ffn("build"), ph_l1ffn),
            ("q6", w_l1ab, ph_l1ab), ("q7", w_l1ffn("load"), ph_l1ffn)][:nphases]

    for i, (pref, wfn, pfn) in enumerate(plan):
        ctx.begin_phase(pref + "_")
        with nc.Block(pref, no_gpsimd_drain=True) as blk:
            @blk.sync
            def _(sp, i=i, wfn=wfn, pfn=pfn):
                ctx.sc("ystore").wait(ctx.sp)
                wfn()
                for b in range(NB):
                    emit_ln(ctx, stream[i], b)
                    ctx.sc("ln_s").wait(ctx.t)
                    ctx.sc("ln_tr").wait(ctx.t)
                    pfn(b, stream[i], stream[i + 1])
        ctx.end_phase()

    with nc.Block("fin", no_gpsimd_drain=True) as blk:
        @blk.sync
        def _(sp):
            ctx.sc("ystore").wait(ctx.sp)
    return nc



_FUSED = [None]

def kernel(**inputs):
    from concourse.bass_utils import run_bass_kernel_spmd
    if _FUSED[0] is None:
        _FUSED[0] = build_fused_program(nphases=8)
    nc = _FUSED[0]
    cons = host_constants()
    n = 8
    in_maps = []
    for c in range(n):
        im = dict(cons)
        for k, v_ in inputs.items():
            if k != "L0_attn_a_x":
                im[k] = np.asarray(v_)
        im["x_in"] = np.ascontiguousarray(inputs["L0_attn_a_x"][4 * c:4 * c + 4])
        in_maps.append(im)
    res = run_bass_kernel_spmd(nc, in_maps, core_ids=list(range(n)))
    return np.concatenate([np.asarray(res.results[c]["x_out"]) for c in range(n)], axis=0)


# version 08 (final)
# speedup vs baseline: 1.0298x; 1.0298x over previous
import numpy as np
import ml_dtypes
import concourse.bass as bass
import concourse.mybir as mybir

F32 = mybir.dt.float32
BF16 = mybir.dt.bfloat16
AF = mybir.ActivationFunctionType
ALU = mybir.AluOpType
AX = mybir.AxisListType

BF = ml_dtypes.bfloat16
FENCE_L0AA = False
L0AA_PARTS = 7
L0AA_DIVMODE = 0
T = 2048
D = 1024
NB = 4
NTT = T // 128
NK = D // 128


def host_constants():
    d = np.arange(1024, dtype=np.float64)
    ang = 2 * np.pi * np.outer(d, d) / 1024.0
    C1024 = np.cos(ang).astype(np.float32)
    S1024 = np.sin(ang).astype(np.float32)
    n = np.arange(128, dtype=np.float64)
    ang2 = 2 * np.pi * np.outer(n, n) / 128.0
    C128 = np.cos(ang2).astype(np.float32)
    S128 = np.sin(ang2).astype(np.float32)
    eye = np.eye(128, dtype=np.float32)
    i_idx = np.arange(512)[None, :]
    j_idx = np.arange(128)[:, None]
    masks = np.stack([(i_idx - 128 * r - j_idx >= 0) for r in range(4)]).astype(np.float32)
    d0 = (i_idx - j_idx).astype(np.float32) * np.ones((128, 1), np.float32)
    deltas = np.stack([d0] + [np.maximum(d0 - 128 * r, 0) for r in range(4)])
    cons = {
        "cC1024": C1024.astype(BF), "cS1024": S1024.astype(BF), "cS1024n": (-S1024).astype(BF),
        "cC128": C128.astype(BF), "cS128": S128.astype(BF), "cS128n": (-S128).astype(BF),
        "cEye": eye.astype(BF),
        "cEyeF": eye,
        "cMasks": masks.astype(BF),
        "cDeltas": deltas.astype(np.float16),
        "cOffs": (-384.0 + 128.0 * np.arange(20, dtype=np.float32)).reshape(1, 20),
    }
    return cons



class SC:
    def __init__(self, nc, name):
        self.h = nc.alloc_semaphore(name)
        self.n = 0
    def inc(self, instr, k=1):
        instr.then_inc(self.h, k)
        self.n += k
        return instr
    def wait(self, eng, val=None):
        eng.wait_ge(self.h, self.n if val is None else val)


class Bank:
    def __init__(self, nc, name, shape=(128, 512), dtype=F32):
        self.t = nc.alloc_psum_tensor(name, list(shape), dtype)
        self._sc = SC(nc, name + "_f")
        self.rel = None
        self.pending = False
    def begin(self, eng):
        assert not self.pending, "bank not released"
        if self.rel is not None:
            eng.wait_ge(self.rel[0], self.rel[1])
        self.pending = True
        return self.t
    def end(self, instr, sc=None):
        assert self.pending
        sc = sc or self._sc
        sc.inc(instr)
        self.rel = (sc.h, sc.n)
        self.pending = False
        return instr
    def release_at(self, sc):
        assert self.pending
        self.rel = (sc.h, sc.n)
        self.pending = False


class Ctx:
    def __init__(self, nc, cons_shapes=True):
        self.nc = nc
        self.t = nc.tensor
        self.v = nc.vector
        self.s = nc.scalar
        self.g = nc.gpsimd
        self.sp = nc.sync
        self._sc = {}
        self.cons = {}
        for name, shp, dt in [("cC1024", [1024, 1024], BF16), ("cS1024", [1024, 1024], BF16),
                              ("cS1024n", [1024, 1024], BF16), ("cC128", [128, 128], BF16),
                              ("cS128", [128, 128], BF16), ("cS128n", [128, 128], BF16),
                              ("cEye", [128, 128], BF16), ("cEyeF", [128, 128], F32),
                              ("cMasks", [4, 128, 512], BF16),
                              ("cDeltas", [5, 128, 512], mybir.dt.float16),
                              ("cOffs", [1, 20], F32)]:
            self.cons[name] = nc.dram_tensor(name, shp, dt, kind="ExternalInput").ap()
        self.eye = nc.alloc_sbuf_tensor("eye", [128, 128], BF16)
        self.eye_f = nc.alloc_sbuf_tensor("eye_f", [128, 128], F32)
        self.eps = nc.alloc_sbuf_tensor("eps", [128, 1], F32)
        self.halfpi = nc.alloc_sbuf_tensor("halfpi", [128, 1], F32)
        self.onesf = nc.alloc_sbuf_tensor("onesf", [128, 1], F32)
        self.xarena = nc.alloc_sbuf_tensor("xarena", [128, 2048], F32)
        self.lns = nc.alloc_sbuf_tensor("lns", [128, 96], F32)
        self.x_r = [nc.alloc_sbuf_tensor(f"x_r{i}", [128, 1024], F32) for i in range(2)]
        self.sq_tr = nc.alloc_sbuf_tensor("sq_tr", [128, 1024], BF16)
        self.stat = nc.alloc_sbuf_tensor("stat", [128, 8], F32)
        self.xn_tm = nc.alloc_sbuf_tensor("xn_tm", [128, 1024], BF16)
        self.xn = nc.alloc_sbuf_tensor("xn", [128, NK * T], BF16)
        self.y_tmp = nc.alloc_sbuf_tensor("y_tmp", [128, 4096], BF16)
        self.pb = [Bank(nc, f"pb{i}") for i in range(4)]
        self.hold = [Bank(nc, f"ph{i}") for i in range(2)]
        self.trb = [Bank(nc, f"trb{i}", (128, 1024), BF16) for i in range(2)]
        self._pb_i = 0
    def bank(self):
        b = self.pb[self._pb_i % len(self.pb)]
        self._pb_i += 1
        return b
    def sc(self, name):
        if name not in self._sc:
            self._sc[name] = SC(self.nc, name)
        return self._sc[name]

    def fence_all(self):
        for e in (self.t, self.v, self.s, self.sp, self.g):
            for sc in self._sc.values():
                if sc.n:
                    e.wait_ge(sc.h, sc.n)

    def begin_phase(self, prefix):
        import contextlib
        self._stack = contextlib.ExitStack()
        self._prefix = prefix

    def alloc(self, name, shape, dtype):
        if getattr(self, "_stack", None) is not None:
            return self._stack.enter_context(self.nc.sbuf_tensor(self._prefix + name, shape, dtype))
        return self.nc.alloc_sbuf_tensor(name, shape, dtype)

    def end_phase(self):
        if getattr(self, "_stack", None) is not None:
            self._stack.close()
            self._stack = None

    def emit_const_init(self):
        c = self.sc("consts")
        c.inc(self.sp.dma_start(out=self.eye[:], in_=self.cons["cEye"]), 16)
        c.inc(self.sp.dma_start(out=self.eye_f[:], in_=self.cons["cEyeF"]), 16)
        self.g.memset(self.eps[:], 1e-5)
        self.g.memset(self.halfpi[:], float(np.pi / 2))
        c.inc(self.g.memset(self.onesf[:], 1.0))
        for e in (self.t, self.v, self.s, self.sp):
            c.wait(e)


def emit_ln(ctx, x_hbm, b):
    nc, t, v, s, sp, g = ctx.nc, ctx.t, ctx.v, ctx.s, ctx.sp, ctx.g
    ld = ctx.sc("ln_ld")
    vst = ctx.sc("ln_v")
    sst = ctx.sc("ln_s")
    trc = ctx.sc("ln_tr")
    hp = ctx.sc("ln_hop")
    L = ctx.lns
    SUM, SSQ, MEAN, RSTD, NMR, NM2, VAR = 0, 16, 32, 48, 64, 80, 88
    for half in range(8):
        c0 = half * 2
        sp.wait_ge(sst.h, sst.n)
        sp.wait_ge(vst.h, vst.n)
        last_red = None
        for j in range(2):
            tt = c0 + j
            xt = ctx.xarena[:, j * 1024:(j + 1) * 1024]
            ld.inc(sp.dma_start(out=xt, in_=x_hbm[b, tt * 128:(tt + 1) * 128, :]), 16)
            ld.wait(v)
            last_red = v.tensor_reduce(L[:, SUM + tt:SUM + tt + 1], xt, AX.X, ALU.add)
            ld.wait(s)
            vst.wait(s)
            sst.inc(s.activation(ctx.sq_tr[:], xt, AF.Square))
            sst.wait(v)
            last_red = v.tensor_reduce(L[:, SSQ + tt:SSQ + tt + 1], ctx.sq_tr[:], AX.X, ALU.add)
            vst.inc(last_red)
        vst.wait(v)
        sst.wait(v)
        ms = slice(MEAN + c0, MEAN + c0 + 2)
        hp.inc(v.tensor_scalar(L[:, ms], L[:, SUM + c0:SUM + c0 + 2], 1.0 / D, None, ALU.mult))
        hp.wait(v)
        hp.inc(v.scalar_tensor_tensor(L[:, NM2:NM2 + 2], L[:, ms], -1.0, L[:, ms], ALU.mult, ALU.mult))
        hp.wait(v)
        vst.inc(v.scalar_tensor_tensor(L[:, VAR:VAR + 2], L[:, SSQ + c0:SSQ + c0 + 2], 1.0 / D,
                                       L[:, NM2:NM2 + 2], ALU.mult, ALU.add))
        vst.wait(s)
        s.activation(L[:, NM2:NM2 + 2], L[:, VAR:VAR + 2], AF.Ln, bias=ctx.eps[:])
        sst.inc(s.activation(L[:, RSTD + c0:RSTD + c0 + 2], L[:, NM2:NM2 + 2], AF.Exp, scale=-0.5))
        sst.wait(v)
        vst.inc(v.scalar_tensor_tensor(L[:, NMR + c0:NMR + c0 + 2], L[:, ms], -1.0,
                                       L[:, RSTD + c0:RSTD + c0 + 2], ALU.mult, ALU.mult))
        vst.wait(s)
        for j in range(2):
            tt = c0 + j
            xt = ctx.xarena[:, j * 1024:(j + 1) * 1024]
            trc.wait(s)
            sst.inc(s.activation(ctx.xn_tm[:], xt, AF.Identity,
                                 bias=L[:, NMR + tt:NMR + tt + 1], scale=L[:, RSTD + tt:RSTD + tt + 1]))
            trb = ctx.trb[tt % 2]
            sst.wait(t)
            ps = trb.begin(t)
            for k in range(NK):
                ins = t.transpose(ps[:, k * 128:(k + 1) * 128], ctx.xn_tm[:, k * 128:(k + 1) * 128], ctx.eye[:])
            trc.inc(ins)
            trc.wait(v)
            dst = ctx.xn[:].rearrange("p (k t) -> p k t", t=T)[:, :, tt * 128: tt * 128 + 128]
            src3 = ps[:].rearrange("p (k c) -> p k c", c=128)
            trb.end(v.tensor_copy(dst, src3), trc)


def emit_finale_tg(ctx, x_hbm, y_hbm, b, tg, y_fm, bias_tile=None):
    nc, t, v, s, sp = ctx.nc, ctx.t, ctx.v, ctx.s, ctx.sp
    ld = ctx.sc("fin_ld")
    fv = ctx.sc("fin_v")
    yw = ctx.sc("ystore")
    for ts in range(4):
        tok0 = tg * 512 + ts * 128
        xr = ctx.x_r[ts % 2]
        sp.wait_ge(fv.h, fv.n)
        sp.wait_ge(yw.h, yw.n)
        ld.inc(sp.dma_start(out=xr[:], in_=x_hbm[b, tok0:tok0 + 128, :]), 16)
        trb = ctx.trb[ts % 2]
        ps = trb.begin(t)
        for ot in range(NK):
            ins = t.transpose(ps[:, ot * 128:(ot + 1) * 128], y_fm[:, ot * 512 + ts * 128: ot * 512 + ts * 128 + 128], ctx.eye[:])
        tr = ctx.sc("fin_tr")
        tr.inc(ins)
        ld.wait(v)
        tr.wait(v)
        pst = ps[:].rearrange("p (k c) -> p k c", c=128)
        xr3 = xr[:].rearrange("p (k c) -> p k c", c=128)
        trb.end(v.tensor_tensor(xr3, xr3, pst, ALU.add), fv)
        fv.wait(sp)
        yw.inc(sp.dma_start(out=y_hbm[b, tok0:tok0 + 128, :], in_=xr[:]), 16)
    tr.wait(s)


def emit_finale_tg32(ctx, x_hbm, y_hbm, b, tg, y_fm32):
    nc, t, v, s, sp = ctx.nc, ctx.t, ctx.v, ctx.s, ctx.sp
    ld = ctx.sc("fin_ld")
    fv = ctx.sc("fin_v")
    yw = ctx.sc("ystore")
    tr = ctx.sc("fin_tr")
    for ts in range(4):
        tok0 = tg * 512 + ts * 128
        xr = ctx.x_r[ts % 2]
        sp.wait_ge(fv.h, fv.n)
        sp.wait_ge(yw.h, yw.n)
        ld.inc(sp.dma_start(out=xr[:], in_=x_hbm[b, tok0:tok0 + 128, :]), 16)
        for half in range(2):
            bk = ctx.bank()
            ps = bk.begin(t)
            for oi in range(4):
                ot = half * 4 + oi
                ins = t.transpose(ps[:, oi * 128:(oi + 1) * 128],
                                  y_fm32[:, ot * 512 + ts * 128: ot * 512 + ts * 128 + 128],
                                  ctx.eye_f[:])
            tr.inc(ins)
            ld.wait(v)
            tr.wait(v)
            pst = ps[:].rearrange("p (k c) -> p k c", c=128)
            xr3 = xr[:, half * 512:(half + 1) * 512].rearrange("p (k c) -> p k c", c=128)
            bk.end(v.tensor_tensor(xr3, xr3, pst, ALU.add), fv)
        fv.wait(sp)
        yw.inc(sp.dma_start(out=y_hbm[b, tok0:tok0 + 128, :], in_=xr[:]), 16)
    ctx.sc("fin_tr").wait(s)



def emit_l1ffn_weights(ctx, wt, cache=None, mode="build"):
    nc, v, s, sp, g = ctx.nc, ctx.v, ctx.s, ctx.sp, ctx.g
    ctx.cosT = ctx.alloc("cosT", [128, 8192], BF16)
    ctx.sinP = ctx.alloc("sinP", [128, 8192], BF16)
    ctx._e1 = ctx.alloc("e1", [128, 512], F32)
    ctx._e2 = ctx.alloc("e2", [128, 512], F32)
    ctx._e3 = ctx.alloc("e3", [128, 512], F32)
    ctx._e4 = ctx.alloc("e4", [128, 512], F32)
    ld = ctx.sc("w_ld")
    if mode == "load":
        ld.inc(sp.dma_start(out=ctx.cosT[:], in_=cache[0]), 16)
        ld.inc(sp.dma_start(out=ctx.sinP[:], in_=cache[1]), 16)
        for e in (ctx.t, ctx.s, ctx.v):
            ld.wait(e)
        return
    th = ctx.alloc("th_t", [128, 1024], F32)
    tn = ctx.alloc("tn_t", [128, 1024], F32)
    zz = ctx.alloc("zz_t", [128, 1024], F32)
    wv = ctx.sc("w_v")
    ws = ctx.sc("w_s")
    for (w0, w1, dst, use_bias) in [(wt["theta"], wt["theta_noise"], ctx.cosT, True),
                                    (wt["phi"], wt["phi_noise"], ctx.sinP, False)]:
        for kt in range(8):
            ws.wait(sp)
            ld.inc(sp.dma_start(out=th[:], in_=w0[kt * 128:(kt + 1) * 128, :]), 16)
            ld.inc(sp.dma_start(out=tn[:], in_=w1[kt * 128:(kt + 1) * 128, :]), 16)
            ld.wait(ctx.v)
            wv.inc(ctx.v.tensor_tensor(zz[:], th[:], tn[:], ALU.add))
            wv.wait(ctx.s)
            ins = ctx.s.activation(dst[:, kt * 1024:(kt + 1) * 1024], zz[:], AF.Sin,
                                   bias=(ctx.halfpi[:] if use_bias else 0.0))
            ws.inc(ins)
    if cache is not None:
        ws.wait(sp)
        ld.inc(sp.dma_start(out=cache[0], in_=ctx.cosT[:]), 16)
        ld.inc(sp.dma_start(out=cache[1], in_=ctx.sinP[:]), 16)
    for e in (ctx.t, ctx.s, ctx.v):
        ld.wait(e)
        ctx.sc("w_s").wait(e)


def emit_l1ffn_b(ctx, x_hbm, y_hbm, b):
    nc, t, v, s = ctx.nc, ctx.t, ctx.v, ctx.s
    e1, e2, e3, e4 = ctx._e1, ctx._e2, ctx._e3, ctx._e4
    mm = ctx.sc("ffn_mm")
    se = ctx.sc("ffn_s")
    ve = ctx.sc("ffn_v")
    for tg in range(4):
        for ot in range(8):
            prb = ctx.bank()
            pr = prb.begin(t)
            for kt in range(8):
                ins = t.matmul(pr[:], ctx.cosT[:, kt * 1024 + ot * 128: kt * 1024 + ot * 128 + 128],
                               ctx.xn[:, kt * T + tg * 512: kt * T + tg * 512 + 512],
                               start=(kt == 0), stop=(kt == 7))
            mm.inc(ins)
            pib = ctx.bank()
            pi = pib.begin(t)
            for kt in range(8):
                ins = t.matmul(pi[:], ctx.sinP[:, kt * 1024 + ot * 128: kt * 1024 + ot * 128 + 128],
                               ctx.xn[:, kt * T + tg * 512: kt * T + tg * 512 + 512],
                               start=(kt == 0), stop=(kt == 7))
            mm.inc(ins)
            mm.wait(s)
            ve.wait(s)
            for bank, psv, dst in [(prb, pr, e1), (pib, pi, e2)]:
                ins = s.activation(dst[:], psv[:], AF.Square)
                bank.end(ins, se)
            se.wait(v)
            ve.inc(v.tensor_tensor(e3[:], e1[:], e2[:], ALU.add))
            ve.wait(s)
            s.activation(e4[:], e3[:], AF.Ln)
            ins = s.activation(ctx.y_tmp[:, ot * 512:(ot + 1) * 512], e4[:], AF.Exp, scale=0.5)
            se.inc(ins)
        se.wait(t)
        se.wait(v)
        emit_finale_tg(ctx, x_hbm, y_hbm, b, tg, ctx.y_tmp)


def build_l1ffn_program(debug=False):
    nc = bass.Bass("TRN2", target_bir_lowering=False, debug=False)
    x_in = nc.dram_tensor("x_in", [NB, T, D], F32, kind="ExternalInput").ap()
    x_out = nc.dram_tensor("x_out", [NB, T, D], F32, kind="ExternalOutput").ap()
    wt = {}
    for name in ["theta", "phi", "theta_noise", "phi_noise"]:
        wt[name] = nc.dram_tensor(name, [1024, 1024], F32, kind="ExternalInput").ap()
    ctx = Ctx(nc)
    with nc.Block("init", no_gpsimd_drain=True) as blk:
        @blk.sync
        def _(sp):
            ctx.emit_const_init()
            ctx.sc("consts").wait(ctx.s)
            ctx.sc("ginit").wait(ctx.s)
            ctx.sc("consts").wait(ctx.v)
            ctx.sc("ginit").wait(ctx.v)
            ctx.sc("ginit").wait(ctx.sp)
            emit_l1ffn_weights(ctx, wt)
    for b in range(NB):
        with nc.Block(f"b{b}", no_gpsimd_drain=True) as blk:
            @blk.sync
            def _(sp, b=b):
                emit_ln(ctx, x_in, b)
                ctx.sc("ln_s").wait(ctx.t)
                ctx.sc("ln_tr").wait(ctx.t)
                emit_l1ffn_b(ctx, x_in, x_out, b)
    if debug:
        dbg = {}
        for nm, tns in [("d_lng", ctx.lng), ("d_loff", ctx.loff), ("d_QF", ctx.QF), ("d_KF", ctx.KF),
                        ("d_VF", ctx.VF), ("d_gbuf", ctx.gbuf), ("d_AG", ctx.AG), ("d_xn", ctx.xn),
                        ("d_WTq", ctx.WT["q"]), ("d_WTv", ctx.WT["v"]), ("d_WOT", ctx.WOT)]:
            dbg[nm] = nc.dram_tensor(nm, list(tns.shape), tns.dtype, kind="ExternalOutput").ap()
        with nc.Block("dbg", no_gpsimd_drain=True) as blk:
            @blk.sync
            def _(sp):
                dd = ctx.sc("dbgst")
                for nm, tns in [("d_lng", ctx.lng), ("d_loff", ctx.loff), ("d_QF", ctx.QF), ("d_KF", ctx.KF),
                                ("d_VF", ctx.VF), ("d_gbuf", ctx.gbuf), ("d_AG", ctx.AG), ("d_xn", ctx.xn),
                                ("d_WTq", ctx.WT["q"]), ("d_WTv", ctx.WT["v"]), ("d_WOT", ctx.WOT)]:
                    dd.inc(ctx.sp.dma_start(out=dbg[nm], in_=tns[:]), 16)
                dd.wait(ctx.sp)
    with nc.Block("fin", no_gpsimd_drain=True) as blk:
        @blk.sync
        def _(sp):
            ctx.sc("ystore").wait(ctx.sp)
    return nc


def load_bias_cols(ctx, name, b_hbm):
    bt = ctx.alloc(name, [128, 8], F32)
    ld = ctx.sc("bias_ld")
    for k in range(8):
        src = b_hbm[k * 128:(k + 1) * 128].rearrange("(p one) -> p one", one=1)
        ld.inc(ctx.sp.dma_start(out=bt[:, k:k + 1], in_=src), 16)
    return bt


def emit_build_A(ctx, wr_hbm, wi_hbm, cache=None, mode="build"):
    nc, t, v, s, sp, g = ctx.nc, ctx.t, ctx.v, ctx.s, ctx.sp, ctx.g
    ctx.Ab = ctx.alloc("Ab", [128, 8192], BF16)
    ld = ctx.sc("A_ld")
    mm = ctx.sc("A_mm")
    ev = ctx.sc("A_ev")
    ng = ctx.sc("A_ng")
    if mode == "load":
        ld.inc(ctx.sp.dma_start(out=ctx.Ab[:], in_=cache), 16)
        ld.wait(ctx.t)
        ld.wait(ctx.s)
        ld.wait(ctx.v)
        return
    with (nc.sbuf_tensor("Cb", [128, 8192], BF16) as Cb,
          nc.sbuf_tensor("Sb", [128, 8192], BF16) as Sb,
          nc.sbuf_tensor("PT", [128, 8192], BF16) as PT,
          nc.sbuf_tensor("QT", [128, 8192], BF16) as QT,
          nc.sbuf_tensor("WR", [128, 8192], BF16) as WR,
          nc.sbuf_tensor("WI", [128, 8192], BF16) as WI):
        for kt in range(8):
            sl = slice(kt * 1024, (kt + 1) * 1024)
            hb = slice(kt * 128, (kt + 1) * 128)
            ld.inc(sp.dma_start(out=Cb[:, sl], in_=ctx.cons["cC1024"][hb, :]), 16)
            ld.inc(sp.dma_start(out=Sb[:, sl], in_=ctx.cons["cS1024"][hb, :]), 16)
            ld.inc(g.dma_start(out=WR[:, sl], in_=wr_hbm[hb, :]), 16)
            ld.inc(g.dma_start(out=WI[:, sl], in_=wi_hbm[hb, :]), 16)
        ld.wait(t)
        def stage(dst, terms, scale):
            for ot in range(8):
                for dg in range(2):
                    bk = ctx.bank()
                    ps = bk.begin(t)
                    nmm = len(terms) * 8
                    i = 0
                    for (lh, rh) in terms:
                        for ft in range(8):
                            ins = t.matmul(ps[:], lh[:, ft * 1024 + ot * 128: ft * 1024 + ot * 128 + 128],
                                           rh[:, ft * 1024 + dg * 512: ft * 1024 + dg * 512 + 512],
                                           start=(i == 0), stop=(i == nmm - 1))
                            i += 1
                    mm.inc(ins)
                    mm.wait(s)
                    bk.end(s.activation(dst[:, ot * 1024 + dg * 512: ot * 1024 + dg * 512 + 512],
                                        ps[:], AF.Identity, scale=scale), ev)
        stage(PT, [(WR, Cb), (WI, Sb)], 1.0)
        mm.wait(v)
        ng.inc(v.tensor_scalar(WI[:], WI[:], -1.0, None, ALU.mult))
        ng.wait(t)
        stage(QT, [(WR, Sb), (WI, Cb)], 1.0)
        ev.wait(t)
        stage(ctx.Ab, [(PT, Cb), (QT, Sb)], 1.0 / 1024.0)
        ev.wait(t)
        ev.wait(v)
        ev.wait(sp)
        ev.wait(g)
    if cache is not None:
        ev.wait(sp)
        ld.inc(sp.dma_start(out=cache, in_=ctx.Ab[:]), 16)
        ld.wait(sp)


def emit_l0ffn_b(ctx, x_hbm, y_hbm, b, bias_t):
    nc, t, v, s = ctx.nc, ctx.t, ctx.v, ctx.s
    mm = ctx.sc("ffn_mm")
    se = ctx.sc("ffn_s")
    for tg in range(4):
        for ot in range(8):
            bk = ctx.bank()
            ps = bk.begin(t)
            for kt in range(8):
                ins = t.matmul(ps[:], ctx.Ab[:, kt * 1024 + ot * 128: kt * 1024 + ot * 128 + 128],
                               ctx.xn[:, kt * T + tg * 512: kt * T + tg * 512 + 512],
                               start=(kt == 0), stop=(kt == 7))
            mm.inc(ins)
            mm.wait(s)
            bk.end(s.activation(ctx.y_tmp[:, ot * 512:(ot + 1) * 512], ps[:], AF.Identity,
                                bias=bias_t[:, ot:ot + 1]), se)
        se.wait(t)
        se.wait(v)
        emit_finale_tg(ctx, x_hbm, y_hbm, b, tg, ctx.y_tmp)


def build_l0ffn_program(debug=False):
    nc = bass.Bass("TRN2", target_bir_lowering=False, debug=False)
    x_in = nc.dram_tensor("x_in", [NB, T, D], F32, kind="ExternalInput").ap()
    x_out = nc.dram_tensor("x_out", [NB, T, D], F32, kind="ExternalOutput").ap()
    wr = nc.dram_tensor("w_freq_real", [1024, 1024], F32, kind="ExternalInput").ap()
    wi = nc.dram_tensor("w_freq_imag", [1024, 1024], F32, kind="ExternalInput").ap()
    bias = nc.dram_tensor("bias", [1024], F32, kind="ExternalInput").ap()
    ctx = Ctx(nc)
    with nc.Block("init", no_gpsimd_drain=True) as blk:
        @blk.sync
        def _(sp):
            ctx.emit_const_init()
            bias_t = load_bias_cols(ctx, "bias_t", bias)
            ctx.bias_t = bias_t
            emit_build_A(ctx, wr, wi)
            ctx.sc("bias_ld").wait(ctx.s)
    for b in range(NB):
        with nc.Block(f"b{b}", no_gpsimd_drain=True) as blk:
            @blk.sync
            def _(sp, b=b):
                emit_ln(ctx, x_in, b)
                ctx.sc("ln_s").wait(ctx.t)
                ctx.sc("ln_tr").wait(ctx.t)
                emit_l0ffn_b(ctx, x_in, x_out, b, ctx.bias_t)
    if debug:
        dbg = {}
        for nm, tns in [("d_lng", ctx.lng), ("d_loff", ctx.loff), ("d_QF", ctx.QF), ("d_KF", ctx.KF),
                        ("d_VF", ctx.VF), ("d_gbuf", ctx.gbuf), ("d_AG", ctx.AG), ("d_xn", ctx.xn),
                        ("d_WTq", ctx.WT["q"]), ("d_WTv", ctx.WT["v"]), ("d_WOT", ctx.WOT)]:
            dbg[nm] = nc.dram_tensor(nm, list(tns.shape), tns.dtype, kind="ExternalOutput").ap()
        with nc.Block("dbg", no_gpsimd_drain=True) as blk:
            @blk.sync
            def _(sp):
                dd = ctx.sc("dbgst")
                for nm, tns in [("d_lng", ctx.lng), ("d_loff", ctx.loff), ("d_QF", ctx.QF), ("d_KF", ctx.KF),
                                ("d_VF", ctx.VF), ("d_gbuf", ctx.gbuf), ("d_AG", ctx.AG), ("d_xn", ctx.xn),
                                ("d_WTq", ctx.WT["q"]), ("d_WTv", ctx.WT["v"]), ("d_WOT", ctx.WOT)]:
                    dd.inc(ctx.sp.dma_start(out=dbg[nm], in_=tns[:]), 16)
                dd.wait(ctx.sp)
    with nc.Block("fin", no_gpsimd_drain=True) as blk:
        @blk.sync
        def _(sp):
            ctx.sc("ystore").wait(ctx.sp)
    return nc


def emit_l0aa_weights(ctx, wt):
    nc, sp, g = ctx.nc, ctx.sp, ctx.g
    ld = ctx.sc("w_ld")
    ctx.WL = {}
    ctx.WH = {}
    for nm in ("q", "k", "v"):
        ctx.WL[nm] = ctx.alloc(f"WL{nm}", [128, 8 * 256], BF16)
        ctx.WH[nm] = ctx.alloc(f"WH{nm}", [128, 2 * 1024], BF16)
        for kt in range(8):
            ld.inc(g.dma_start(out=ctx.WL[nm][:, kt * 256:(kt + 1) * 256],
                               in_=wt[f"w{nm}_lat"][kt * 128:(kt + 1) * 128, :]), 16)
        for lt in range(2):
            ld.inc(g.dma_start(out=ctx.WH[nm][:, lt * 1024:(lt + 1) * 1024],
                               in_=wt[f"w{nm}_heads"][lt * 128:(lt + 1) * 128, :]), 16)
    ctx.WP = ctx.alloc("WP", [128, 8192], BF16)
    for kt in range(8):
        ld.inc(g.dma_start(out=ctx.WP[:, kt * 1024:(kt + 1) * 1024],
                           in_=wt["w_proj"][kt * 128:(kt + 1) * 128, :]), 16)
    ctx.bias_t = load_bias_cols(ctx, "bias_t", wt["b_proj"])
    ctx.masks = ctx.alloc("masks", [128, 4 * 512], BF16)
    for r in range(4):
        ld.inc(sp.dma_start(out=ctx.masks[:, r * 512:(r + 1) * 512], in_=ctx.cons["cMasks"][r]), 16)
    ctx.ones_bf = ctx.alloc("ones_bf", [128, 64], BF16)
    ctx.ln8n = ctx.alloc("ln8n", [128, 1], F32)
    ctx.QH = ctx.alloc("QH", [128, 2048], BF16)
    ctx.KH = ctx.alloc("KH", [128, 2048], BF16)
    ctx.VH = ctx.alloc("VH", [128, 16 * 65], BF16)
    ctx.xlat = {nm: ctx.alloc(f"xlat{nm}", [128, 2 * 2048], BF16) for nm in ("q", "k", "v")}
    ctx.ATT = ctx.alloc("ATT", [128, NK * T], BF16)
    ctx.s_t = [ctx.alloc(f"s_t{i}", [128, 512], BF16) for i in range(8)]
    ctx.sqx = ctx.alloc("sqx", [128, 512], BF16)
    ctx.q2row = ctx.alloc("q2row", [1, 512], F32)
    ctx.k2row = ctx.alloc("k2row", [1, 2048], F32)
    ctx.smalls = ctx.alloc("smalls", [1, 16], F32)
    ctx.rec_row = ctx.alloc("rec_row", [1, 512], F32)
    ctx.att_s = ctx.alloc("att_s", [128, 512], BF16)
    ctx.rec_bf = ctx.alloc("rec_bf", [1, 512], BF16)
    g.memset(ctx.ones_bf[:], 1.0)
    g.memset(ctx.ln8n[:], float(np.log(0.125)))
    g.memset(ctx.KH[64:65, :], 1.0)
    ld.inc(g.memset(ctx.VH[:].rearrange("p (j c) -> p j c", c=65)[:, :, 64:65], 1.0))
    for e in (ctx.t, ctx.s, ctx.v):
        ld.wait(e)
    ctx.sc("bias_ld").wait(ctx.s)
    ctx.sc("bias_ld").wait(ctx.t)


def emit_l0aa_b(ctx, x_hbm, y_hbm, b):
    nc, t, v, s, sp, g = ctx.nc, ctx.t, ctx.v, ctx.s, ctx.sp, ctx.g
    mm = ctx.sc("aa_mm")
    ev = ctx.sc("aa_ev")
    vv = ctx.sc("aa_vv")
    sc_exp = ctx.sc("aa_exp")
    sc_av = ctx.sc("aa_av")
    sc_msk = ctx.sc("aa_msk")
    if FENCE_L0AA:
        ctx.fence_all()
    vv.wait(t)
    for nm in ("q", "k", "v"):
        for ot2 in range(2):
            for tg in range(4):
                bk = ctx.bank()
                ps = bk.begin(t)
                for kt in range(8):
                    ins = t.matmul(ps[:], ctx.WL[nm][:, kt * 256 + ot2 * 128: kt * 256 + ot2 * 128 + 128],
                                   ctx.xn[:, kt * T + tg * 512: kt * T + tg * 512 + 512],
                                   start=(kt == 0), stop=(kt == 7))
                mm.inc(ins)
                mm.wait(s)
                bk.end(s.activation(ctx.xlat[nm][:, ot2 * 2048 + tg * 512: ot2 * 2048 + tg * 512 + 512],
                                    ps[:], AF.Identity), ev)
    for h in range(16 if L0AA_PARTS >= 1 else 0):
        if FENCE_L0AA:
            ctx.fence_all()
        ev.wait(t)
        sc_av.wait(s)
        sc_av.wait(v)
        for tg in range(4):
            bkk = ctx.bank()
            psk = bkk.begin(t)
            for lt in range(2):
                ins = t.matmul(psk[0:64, :], ctx.WH["k"][:, lt * 1024 + h * 64: lt * 1024 + h * 64 + 64],
                               ctx.xlat["k"][:, lt * 2048 + tg * 512: lt * 2048 + tg * 512 + 512],
                               start=(lt == 0), stop=(lt == 1))
            mm.inc(ins)
            mm.wait(s)
            ev.inc(s.activation(ctx.KH[0:64, tg * 512:(tg + 1) * 512], psk[0:64, :], AF.Identity, scale=0.125))
            ev.inc(s.activation(ctx.sqx[0:64, :], psk[0:64, :], AF.Square))
            bkk.release_at(ev)
            ev.wait(t)
            bk2 = ctx.bank()
            ps2 = bk2.begin(t)
            mm.inc(t.matmul(ps2[0:1, 0:512], ctx.ones_bf[0:64, 0:1], ctx.sqx[0:64, :], start=True, stop=True))
            mm.wait(v)
            bk2.end(v.tensor_copy(ctx.k2row[0:1, tg * 512:(tg + 1) * 512], ps2[0:1, 0:512]), vv)
        vv.wait(v)
        vv.inc(v.tensor_reduce(ctx.smalls[0:1, 0:1], ctx.k2row[:], AX.X, ALU.max))
        vv.wait(s)
        s.activation(ctx.smalls[0:1, 1:2], ctx.smalls[0:1, 0:1], AF.Ln)
        ev.inc(s.activation(ctx.smalls[0:1, 2:3], ctx.smalls[0:1, 1:2], AF.Identity, scale=0.5,
                            bias=ctx.ln8n[0:1, :]))
        for tg in range(4 if L0AA_PARTS >= 2 else 0):
            bkq = ctx.bank()
            psq = bkq.begin(t)
            for lt in range(2):
                ins = t.matmul(psq[0:64, :], ctx.WH["q"][:, lt * 1024 + h * 64: lt * 1024 + h * 64 + 64],
                               ctx.xlat["q"][:, lt * 2048 + tg * 512: lt * 2048 + tg * 512 + 512],
                               start=(lt == 0), stop=(lt == 1))
            mm.inc(ins)
            mm.wait(s)
            ev.inc(s.activation(ctx.QH[0:64, tg * 512:(tg + 1) * 512], psq[0:64, :], AF.Identity))
            ev.inc(s.activation(ctx.sqx[0:64, :], psq[0:64, :], AF.Square))
            bkq.release_at(ev)
            ev.wait(t)
            bq2 = ctx.bank()
            ps2 = bq2.begin(t)
            mm.inc(t.matmul(ps2[0:1, 0:512], ctx.ones_bf[0:64, 0:1], ctx.sqx[0:64, :], start=True, stop=True))
            mm.wait(v)
            vv.wait(v)
            bq2.end(v.tensor_copy(ctx.q2row[:], ps2[0:1, 0:512]), vv)
            vv.wait(s)
            s.activation(ctx.q2row[:], ctx.q2row[:], AF.Ln)
            ev.inc(s.activation(ctx.q2row[:], ctx.q2row[:], AF.Exp, scale=0.5, bias=ctx.smalls[0:1, 2:3]))
            ev.wait(v)
            vv.inc(v.tensor_scalar(ctx.QH[64:65, tg * 512:(tg + 1) * 512], ctx.q2row[:], -1.0, None, ALU.mult))
        for tt in range(NTT):
            bkv = ctx.bank()
            psv = bkv.begin(t)
            ins = None
            for lt in range(2):
                ins = t.matmul(psv[:, 0:64],
                               ctx.xlat["v"][:, lt * 2048 + tt * 128: lt * 2048 + tt * 128 + 128],
                               ctx.WH["v"][:, lt * 1024 + h * 64: lt * 1024 + h * 64 + 64],
                               start=(lt == 0), stop=(lt == 1))
            mm.inc(ins)
            mm.wait(s)
            bkv.end(s.activation(ctx.VH[:, tt * 65: tt * 65 + 64], psv[:, 0:64], AF.Identity), ev)
        ev.wait(t)
        vv.wait(t)
        for ig in range(4):
            njt = 4 * ig + 4
            for jt in range(njt):
                bs = ctx.bank()
                pss = bs.begin(t)
                mm.inc(t.matmul(pss[:], ctx.KH[0:65, jt * 128: jt * 128 + 128],
                                ctx.QH[0:65, ig * 512: ig * 512 + 512], start=True, stop=True))
                stile = ctx.s_t[jt][:] if jt < 8 else ctx.y_tmp[:, (jt - 8) * 512:(jt - 7) * 512]
                mm.wait(s)
                sc_av.wait(s)
                sc_msk.wait(s)
                bs.end(s.activation(stile, pss[0:128, :], AF.Exp), sc_exp)
                r = jt - 4 * ig
                if r >= 0:
                    sc_exp.wait(v)
                    sc_av.wait(v)
                    sc_msk.inc(v.tensor_tensor(stile, stile, ctx.masks[:, r * 512:(r + 1) * 512], ALU.mult))
            sc_exp.wait(t)
            sc_msk.wait(t)
            bo = ctx.hold[0]
            po = bo.begin(t)
            ins = None
            for jt in range(njt):
                stile = ctx.s_t[jt][:] if jt < 8 else ctx.y_tmp[:, (jt - 8) * 512:(jt - 7) * 512]
                ins = t.matmul(po[0:65, :], ctx.VH[:, jt * 65: jt * 65 + 65], stile,
                               start=(jt == 0), stop=(jt == njt - 1))
            sc_av.inc(ins)
            sc_av.wait(v)
            vv.inc(v.reciprocal(ctx.rec_row[:], po[64:65, :]))
            vv.wait(v)
            vv.inc(v.tensor_scalar(ctx.rec_bf[:], ctx.rec_row[:], 1.0, None, ALU.mult))
            vv.wait(s)
            ev.inc(s.activation(ctx.att_s[0:64, :], po[0:64, :], AF.Identity))
            bo.release_at(ev)
            ev.wait(t)
            vv.wait(t)
            br = ctx.bank()
            pr = br.begin(t)
            mm.inc(t.matmul(pr[0:64, :], ctx.ones_bf[0:1, 0:64], ctx.rec_bf[:], start=True, stop=True))
            mm.wait(v)
            ev.wait(v)
            dstp = (h % 2) * 64
            dst = ctx.ATT[dstp:dstp + 64, (h // 2) * 2048 + ig * 512: (h // 2) * 2048 + ig * 512 + 512]
            br.end(v.tensor_tensor(dst, ctx.att_s[0:64, :], pr[0:64, :], ALU.mult), vv)
    if FENCE_L0AA:
        ctx.fence_all()
    vv.wait(t)
    for tg in range(4):
        if FENCE_L0AA:
            ctx.fence_all()
        for ot in range(8):
            bk = ctx.bank()
            ps = bk.begin(t)
            for kt in range(8):
                ins = t.matmul(ps[:], ctx.WP[:, kt * 1024 + ot * 128: kt * 1024 + ot * 128 + 128],
                               ctx.ATT[:, kt * T + tg * 512: kt * T + tg * 512 + 512],
                               start=(kt == 0), stop=(kt == 7))
            mm.inc(ins)
            mm.wait(s)
            bk.end(s.activation(ctx.y_tmp[:, ot * 512:(ot + 1) * 512], ps[:], AF.Identity,
                                bias=ctx.bias_t[:, ot:ot + 1]), ev)
        ev.wait(t)
        ev.wait(v)
        emit_finale_tg(ctx, x_hbm, y_hbm, b, tg, ctx.y_tmp)


def build_l0aa_program(debug=False):
    nc = bass.Bass("TRN2", target_bir_lowering=False, debug=False)
    x_in = nc.dram_tensor("x_in", [NB, T, D], F32, kind="ExternalInput").ap()
    x_out = nc.dram_tensor("x_out", [NB, T, D], F32, kind="ExternalOutput").ap()
    wt = {}
    for nm, shp in [("wq_lat", [1024, 256]), ("wk_lat", [1024, 256]), ("wv_lat", [1024, 256]),
                    ("wq_heads", [256, 1024]), ("wk_heads", [256, 1024]), ("wv_heads", [256, 1024]),
                    ("w_proj", [1024, 1024]), ("b_proj", [1024])]:
        wt[nm] = nc.dram_tensor(nm, shp, F32, kind="ExternalInput").ap()
    ctx = Ctx(nc)
    with nc.Block("init", no_gpsimd_drain=True) as blk:
        @blk.sync
        def _(sp):
            ctx.emit_const_init()
            emit_l0aa_weights(ctx, wt)
    for b in range(NB):
        with nc.Block(f"b{b}", no_gpsimd_drain=True) as blk:
            @blk.sync
            def _(sp, b=b):
                emit_ln(ctx, x_in, b)
                ctx.sc("ln_s").wait(ctx.t)
                ctx.sc("ln_tr").wait(ctx.t)
                emit_l0aa_b(ctx, x_in, x_out, b)
    if debug:
        dbg = {}
        for nm, tns in [("d_lng", ctx.lng), ("d_loff", ctx.loff), ("d_QF", ctx.QF), ("d_KF", ctx.KF),
                        ("d_VF", ctx.VF), ("d_gbuf", ctx.gbuf), ("d_AG", ctx.AG), ("d_xn", ctx.xn),
                        ("d_WTq", ctx.WT["q"]), ("d_WTv", ctx.WT["v"]), ("d_WOT", ctx.WOT)]:
            dbg[nm] = nc.dram_tensor(nm, list(tns.shape), tns.dtype, kind="ExternalOutput").ap()
        with nc.Block("dbg", no_gpsimd_drain=True) as blk:
            @blk.sync
            def _(sp):
                dd = ctx.sc("dbgst")
                for nm, tns in [("d_lng", ctx.lng), ("d_loff", ctx.loff), ("d_QF", ctx.QF), ("d_KF", ctx.KF),
                                ("d_VF", ctx.VF), ("d_gbuf", ctx.gbuf), ("d_AG", ctx.AG), ("d_xn", ctx.xn),
                                ("d_WTq", ctx.WT["q"]), ("d_WTv", ctx.WT["v"]), ("d_WOT", ctx.WOT)]:
                    dd.inc(ctx.sp.dma_start(out=dbg[nm], in_=tns[:]), 16)
                dd.wait(ctx.sp)
    with nc.Block("fin", no_gpsimd_drain=True) as blk:
        @blk.sync
        def _(sp):
            ctx.sc("ystore").wait(ctx.sp)
    return nc


def emit_l0ab_weights(ctx, wt):
    nc, t, v, s, sp, g = ctx.nc, ctx.t, ctx.v, ctx.s, ctx.sp, ctx.g
    ld = ctx.sc("w_ld")
    mmw = ctx.sc("w_mm")
    evw = ctx.sc("w_ev")
    ctx.WT = {}
    for nm in ("q", "k", "v", "g"):
        ctx.WT[nm] = ctx.alloc(f"WT{nm}", [128, 4096], BF16)
    ctx.WOT = ctx.alloc("WOT", [128, 4096], BF16)
    with nc.sbuf_tensor("wstage", [128, 1024], BF16) as wst:
        for nm in ("q", "k", "v", "g"):
            for ot4 in range(4):
                evw.wait(g)
                ld.inc(g.dma_start(out=wst[:], in_=wt["W_" + nm.upper() if nm != "g" else "W_g_w"]
                                   [ot4 * 128:(ot4 + 1) * 128, :]), 16)
                ld.wait(t)
                trb = ctx.trb[ot4 % 2]
                ps = trb.begin(t)
                for kt in range(8):
                    ins = t.transpose(ps[:, (kt % 8) * 128:(kt % 8) * 128 + 128],
                                      wst[:, kt * 128:(kt + 1) * 128], ctx.eye[:])
                mmw.inc(ins)
                mmw.wait(v)
                dst = ctx.WT[nm][:].rearrange("p (k o) -> p k o", o=512)[:, :, ot4 * 128:ot4 * 128 + 128]
                src3 = ps[:].rearrange("p (k c) -> p k c", c=128)
                trb.end(v.tensor_copy(dst, src3), evw)
        for ot in range(8):
            evw.wait(g)
            ld.inc(g.dma_start(out=wst[:, 0:512], in_=wt["W_O"][ot * 128:(ot + 1) * 128, :]), 16)
            ld.wait(t)
            trb = ctx.trb[ot % 2]
            ps = trb.begin(t)
            for kt4 in range(4):
                ins = t.transpose(ps[:, kt4 * 128:kt4 * 128 + 128],
                                  wst[:, kt4 * 128:(kt4 + 1) * 128], ctx.eye[:])
            mmw.inc(ins)
            mmw.wait(v)
            dst = ctx.WOT[:].rearrange("p (k o) -> p k o", o=1024)[:, :, ot * 128:ot * 128 + 128]
            src3 = ps[:].rearrange("p (c4 c) -> p c4 c", c=128)[:, 0:4, :]
            trb.end(v.tensor_copy(dst, src3), evw)
        evw.wait(sp)
        evw.wait(g)
    ctx.bg_t = ctx.alloc("bg_t", [64, 8], F32)
    for hh in range(8):
        srcb = wt["W_g_b"][hh * 64:(hh + 1) * 64].rearrange("(p one) -> p one", one=1)
        ld.inc(sp.dma_start(out=ctx.bg_t[0:64, hh:hh + 1], in_=srcb), 16)
    ctx.masks = ctx.alloc("masks", [128, 4 * 512], BF16)
    for r in range(4):
        ld.inc(sp.dma_start(out=ctx.masks[:, r * 512:(r + 1) * 512], in_=ctx.cons["cMasks"][r]), 16)
    ctx.deltas = ctx.alloc("deltas", [128, 5 * 512], mybir.dt.float16)
    for r in range(5):
        ld.inc(sp.dma_start(out=ctx.deltas[:, r * 512:(r + 1) * 512], in_=ctx.cons["cDeltas"][r]), 16)
    ctx.lng = ctx.alloc("lng", [128, 8], F32)
    ctx.loff = ctx.alloc("loff", [128, 8 * 20], F32)
    ctx.dl_row = ctx.alloc("dl_row", [1, 32], F32)
    ctx.offs = ctx.alloc("offs", [1, 20], F32)
    ctx.loff_row = ctx.alloc("loff_row", [1, 160], F32)
    ld.inc(sp.dma_start(out=ctx.dl_row[0:1, 0:8], in_=wt["decay_log"].rearrange("(a k) -> a k", a=1)), 16)
    ld.inc(sp.dma_start(out=ctx.offs[:], in_=ctx.cons["cOffs"]), 16)
    ld.wait(s)
    ld.wait(v)
    s.activation(ctx.dl_row[0:1, 8:16], ctx.dl_row[0:1, 0:8], AF.Sigmoid)
    evw.inc(s.activation(ctx.dl_row[0:1, 16:24], ctx.dl_row[0:1, 8:16], AF.Ln))
    evw.wait(v)
    hpw = ctx.sc("w_hop")
    for h in range(8):
        hpw.inc(v.tensor_scalar(ctx.loff_row[0:1, h * 20:(h + 1) * 20], ctx.offs[:],
                                ctx.dl_row[0:1, 16 + h:17 + h], None, ALU.mult))
    gg = ctx.sc("w_g")
    bsc = nc.dram_tensor("bscratch", [1, 256], F32).ap()
    hpw.wait(sp)
    evw.wait(sp)
    gg.inc(sp.dma_start(out=bsc[0:1, 0:8], in_=ctx.dl_row[0:1, 16:24]), 16)
    gg.inc(sp.dma_start(out=bsc[0:1, 64:224], in_=ctx.loff_row[:]), 16)
    gg.wait(sp)
    gg.inc(sp.dma_start(out=ctx.lng[:], in_=bsc[0:1, 0:8].partition_broadcast(128).squeeze(1)), 16)
    gg.inc(sp.dma_start(out=ctx.loff[:], in_=bsc[0:1, 64:224].partition_broadcast(128).squeeze(1)), 16)
    ctx.QF = ctx.alloc("QF", [128, 4 * T], BF16)
    ctx.KF = ctx.alloc("KF", [128, 4 * T], BF16)
    ctx.VF = ctx.alloc("VF", [128, 16 * 520], BF16)
    ctx.AG = ctx.alloc("AG", [128, 4 * T], BF16)
    ctx.gbuf = ctx.alloc("gbuf", [128, 2048], BF16)
    ctx.gamb = ctx.alloc("gamb", [128, 512], BF16)
    ctx.s_t = [ctx.alloc(f"s_t{i}", [128, 512], BF16) for i in range(8)]
    ctx.eb1 = ctx.alloc("eb1", [128, 512], F32)
    ctx.eb2 = ctx.alloc("eb2", [128, 512], F32)
    ctx.rec_row = ctx.alloc("rec_row", [1, 512], F32)
    ctx.rec_bf = ctx.alloc("rec_bf", [1, 512], BF16)
    ctx.att_s = ctx.alloc("att_s", [128, 512], BF16)
    ctx.att_g = ctx.alloc("att_g", [128, 512], BF16)
    ctx.ones_bf = ctx.alloc("ones_bf", [128, 64], BF16)
    g.memset(ctx.ones_bf[:], 1.0)
    last_ms = None
    for hh in range(8):
        last_ms = g.memset(ctx.VF[:].rearrange("p (j q) -> p j q", q=520)[:, :, hh * 65 + 64:hh * 65 + 65], 1.0)
    gg.inc(last_ms)
    for e in (ctx.t, ctx.s, ctx.v):
        ld.wait(e); gg.wait(e); evw.wait(e); ctx.sc("bias_ld").wait(e)


def load_bias_cols_n(ctx, name, b_hbm, ncols):
    bt = ctx.alloc(name, [128, ncols], F32)
    ld = ctx.sc("bias_ld")
    for k in range(ncols):
        src = b_hbm[k * 128:(k + 1) * 128].rearrange("(p one) -> p one", one=1)
        ld.inc(ctx.sp.dma_start(out=bt[:, k:k + 1], in_=src), 16)
    return bt


def emit_l0ab_b(ctx, x_hbm, y_hbm, b):
    nc, t, v, s, sp, g = ctx.nc, ctx.t, ctx.v, ctx.s, ctx.sp, ctx.g
    mm = ctx.sc("ab_mm")
    ev = ctx.sc("ab_ev")
    vv = ctx.sc("ab_vv")
    sc_exp = ctx.sc("ab_exp")
    sc_av = ctx.sc("ab_av")
    hp = ctx.sc("ab_hop")
    vv.wait(t)
    for nm, dstT in (("q", ctx.QF), ("k", ctx.KF)):
        for ot4 in range(4):
            for tg in range(4):
                bk = ctx.bank()
                ps = bk.begin(t)
                for kt in range(8):
                    ins = t.matmul(ps[:], ctx.WT[nm][:, kt * 512 + ot4 * 128: kt * 512 + ot4 * 128 + 128],
                                   ctx.xn[:, kt * T + tg * 512: kt * T + tg * 512 + 512],
                                   start=(kt == 0), stop=(kt == 7))
                mm.inc(ins)
                mm.wait(v)
                ev.wait(v)
                vv.wait(v)
                v.tensor_scalar(ctx.eb1[:], ps[:], 0.0, None, ALU.min)
                vv.inc(v.tensor_scalar(ctx.eb2[:], ps[:], 0.0, None, ALU.max))
                bk.release_at(vv)
                vv.wait(s)
                ev.inc(s.activation(ctx.eb1[:], ctx.eb1[:], AF.Exp))
                ev.wait(v)
                vv.inc(v.tensor_tensor(dstT[:, ot4 * T + tg * 512: ot4 * T + tg * 512 + 512],
                                       ctx.eb1[:], ctx.eb2[:], ALU.add))
    for tt in range(NTT):
        bk = ctx.bank()
        ps = bk.begin(t)
        for kt in range(8):
            ins = t.matmul(ps[:], ctx.xn[:, kt * T + tt * 128: kt * T + tt * 128 + 128],
                           ctx.WT["v"][:, kt * 512:(kt + 1) * 512],
                           start=(kt == 0), stop=(kt == 7))
        mm.inc(ins)
        mm.wait(s)
        dst = ctx.VF[:, tt * 520:(tt + 1) * 520].rearrange("p (h c) -> p h c", c=65)[:, :, 0:64]
        src3 = ps[:].rearrange("p (h c) -> p h c", c=64)
        bk.end(s.activation(dst, src3, AF.Identity), ev)
    for h in range(8):
        rows = slice((h % 2) * 64, (h % 2) * 64 + 64)
        cblk = (h // 2) * T
        ev.wait(t)
        sc_av.wait(t)
        vv.wait(s)
        for ig in range(4):
            bk = ctx.bank()
            ps = bk.begin(t)
            for kt in range(8):
                ins = t.matmul(ps[0:64, :], ctx.WT["g"][:, kt * 512 + h * 64: kt * 512 + h * 64 + 64],
                               ctx.xn[:, kt * T + ig * 512: kt * T + ig * 512 + 512],
                               start=(kt == 0), stop=(kt == 7))
            mm.inc(ins)
            mm.wait(s)
            bk.end(s.activation(ctx.gbuf[0:64, ig * 512:(ig + 1) * 512], ps[0:64, :], AF.Sigmoid,
                                bias=ctx.bg_t[0:64, h:h + 1]), ev)
        for ig in range(4):
            njt = 4 * ig + 4
            for jt in range(njt):
                bs = ctx.bank()
                pss = bs.begin(t)
                mm.inc(t.matmul(pss[:], ctx.KF[rows, cblk + jt * 128: cblk + jt * 128 + 128],
                                ctx.QF[rows, cblk + ig * 512: cblk + ig * 512 + 512],
                                start=True, stop=True))
                r = jt - 4 * ig
                ev.wait(s)
                vv.wait(s)
                sc_exp.wait(s)
                if r >= 0:
                    ev.inc(s.activation(ctx.gamb[:], ctx.deltas[:, (r + 1) * 512:(r + 2) * 512],
                                        AF.Exp, scale=ctx.lng[:, h:h + 1]))
                else:
                    off_i = 4 * ig - jt + 3
                    ev.inc(s.activation(ctx.gamb[:], ctx.deltas[:, 0:512], AF.Exp,
                                        scale=ctx.lng[:, h:h + 1],
                                        bias=ctx.loff[:, h * 20 + off_i:h * 20 + off_i + 1]))
                stile = ctx.s_t[jt][:] if jt < 8 else ctx.y_tmp[:, (jt - 8) * 512:(jt - 7) * 512]
                mm.wait(v)
                ev.wait(v)
                sc_av.wait(v)
                ins = v.scalar_tensor_tensor(stile, pss[:], 1.0, ctx.gamb[:], ALU.mult, ALU.mult)
                if r >= 0:
                    hp.inc(ins)
                    hp.wait(v)
                    ins = v.tensor_tensor(stile, stile, ctx.masks[:, r * 512:(r + 1) * 512], ALU.mult)
                bs.end(ins, sc_exp)
            sc_exp.wait(t)
            bo = ctx.hold[0]
            po = bo.begin(t)
            ins = None
            for jt in range(njt):
                stile = ctx.s_t[jt][:] if jt < 8 else ctx.y_tmp[:, (jt - 8) * 512:(jt - 7) * 512]
                ins = t.matmul(po[0:65, :], ctx.VF[:, jt * 520 + h * 65: jt * 520 + h * 65 + 65],
                               stile, start=(jt == 0), stop=(jt == njt - 1))
            sc_av.inc(ins)
            sc_av.wait(v)
            vv.inc(v.tensor_scalar(ctx.rec_row[:], po[64:65, :], 1e-6, None, ALU.add))
            vv.wait(v)
            vv.inc(v.reciprocal(ctx.rec_row[:], ctx.rec_row[:]))
            vv.wait(v)
            vv.inc(v.tensor_scalar(ctx.rec_bf[:], ctx.rec_row[:], 1.0, None, ALU.mult))
            vv.wait(s)
            ev.inc(s.activation(ctx.att_s[0:64, :], po[0:64, :], AF.Identity))
            bo.release_at(ev)
            ev.wait(t)
            vv.wait(t)
            br = ctx.bank()
            pr = br.begin(t)
            mm.inc(t.matmul(pr[0:64, :], ctx.ones_bf[0:1, 0:64], ctx.rec_bf[:], start=True, stop=True))
            mm.wait(v)
            ev.wait(v)
            hp.inc(v.tensor_tensor(ctx.att_g[0:64, :], ctx.att_s[0:64, :], pr[0:64, :], ALU.mult))
            br.release_at(hp)
            hp.wait(v)
            dst = ctx.AG[rows, cblk + ig * 512: cblk + ig * 512 + 512]
            vv.inc(v.tensor_tensor(dst, ctx.att_g[0:64, :], ctx.gbuf[0:64, ig * 512:(ig + 1) * 512], ALU.mult))
    vv.wait(t)
    for tg in range(4):
        for ot in range(8):
            bk = ctx.bank()
            ps = bk.begin(t)
            for kt4 in range(4):
                ins = t.matmul(ps[:], ctx.WOT[:, kt4 * 1024 + ot * 128: kt4 * 1024 + ot * 128 + 128],
                               ctx.AG[:, kt4 * T + tg * 512: kt4 * T + tg * 512 + 512],
                               start=(kt4 == 0), stop=(kt4 == 3))
            mm.inc(ins)
            mm.wait(s)
            bk.end(s.activation(ctx.y_tmp[:, ot * 512:(ot + 1) * 512], ps[:], AF.Identity), ev)
        ev.wait(t)
        ev.wait(v)
        emit_finale_tg(ctx, x_hbm, y_hbm, b, tg, ctx.y_tmp)


def build_l0ab_program(debug=False):
    nc = bass.Bass("TRN2", target_bir_lowering=False, debug=False)
    x_in = nc.dram_tensor("x_in", [NB, T, D], F32, kind="ExternalInput").ap()
    x_out = nc.dram_tensor("x_out", [NB, T, D], F32, kind="ExternalOutput").ap()
    wt = {}
    for nm, shp in [("W_Q", [512, 1024]), ("W_K", [512, 1024]), ("W_V", [512, 1024]),
                    ("W_g_w", [512, 1024]), ("W_g_b", [512]), ("W_O", [1024, 512]),
                    ("decay_log", [8])]:
        wt[nm] = nc.dram_tensor(nm, shp, F32, kind="ExternalInput").ap()
    ctx = Ctx(nc)
    with nc.Block("init", no_gpsimd_drain=True) as blk:
        @blk.sync
        def _(sp):
            ctx.emit_const_init()
            emit_l0ab_weights(ctx, wt)
    for b in range(NB):
        with nc.Block(f"b{b}", no_gpsimd_drain=True) as blk:
            @blk.sync
            def _(sp, b=b):
                emit_ln(ctx, x_in, b)
                ctx.sc("ln_s").wait(ctx.t)
                ctx.sc("ln_tr").wait(ctx.t)
                emit_l0ab_b(ctx, x_in, x_out, b)
    if debug:
        dbg = {}
        for nm, tns in [("d_lng", ctx.lng), ("d_loff", ctx.loff), ("d_QF", ctx.QF), ("d_KF", ctx.KF),
                        ("d_VF", ctx.VF), ("d_gbuf", ctx.gbuf), ("d_AG", ctx.AG), ("d_xn", ctx.xn),
                        ("d_WTq", ctx.WT["q"]), ("d_WTv", ctx.WT["v"]), ("d_WOT", ctx.WOT)]:
            dbg[nm] = nc.dram_tensor(nm, list(tns.shape), tns.dtype, kind="ExternalOutput").ap()
        with nc.Block("dbg", no_gpsimd_drain=True) as blk:
            @blk.sync
            def _(sp):
                dd = ctx.sc("dbgst")
                for nm, tns in [("d_lng", ctx.lng), ("d_loff", ctx.loff), ("d_QF", ctx.QF), ("d_KF", ctx.KF),
                                ("d_VF", ctx.VF), ("d_gbuf", ctx.gbuf), ("d_AG", ctx.AG), ("d_xn", ctx.xn),
                                ("d_WTq", ctx.WT["q"]), ("d_WTv", ctx.WT["v"]), ("d_WOT", ctx.WOT)]:
                    dd.inc(ctx.sp.dma_start(out=dbg[nm], in_=tns[:]), 16)
                dd.wait(ctx.sp)
    with nc.Block("fin", no_gpsimd_drain=True) as blk:
        @blk.sync
        def _(sp):
            ctx.sc("ystore").wait(ctx.sp)
    return nc


def emit_l1ab_weights(ctx, wt):
    nc, t, v, s, sp, g = ctx.nc, ctx.t, ctx.v, ctx.s, ctx.sp, ctx.g
    ld = ctx.sc("w_ld")
    ctx.WOUT = ctx.alloc("WOUT", [128, 8192], BF16)
    for kt in range(8):
        ld.inc(g.dma_start(out=ctx.WOUT[:, kt * 1024:(kt + 1) * 1024],
                           in_=wt["w_out"][kt * 128:(kt + 1) * 128, :]), 16)
    ctx.C128 = ctx.alloc("C128s", [128, 128], BF16)
    ctx.S128 = ctx.alloc("S128s", [128, 128], BF16)
    ctx.S128n = ctx.alloc("S128ns", [128, 128], BF16)
    ld.inc(sp.dma_start(out=ctx.C128[:], in_=ctx.cons["cC128"]), 16)
    ld.inc(sp.dma_start(out=ctx.S128[:], in_=ctx.cons["cS128"]), 16)
    ld.inc(sp.dma_start(out=ctx.S128n[:], in_=ctx.cons["cS128n"]), 16)
    ctx.whd = ctx.alloc("whd", [128, 3 * 1024], BF16)
    ctx.pl = [ctx.alloc(f"pl{i}", [128, T], F32) for i in range(7)]
    ctx.plG = ctx.pl[6]
    ctx.ones2k = ctx.alloc("ones2k", [128, T], F32)
    ld.inc(g.memset(ctx.ones2k[:], 1.0))
    ctx.wrb = ctx.alloc("wrb", [128, T], BF16)
    ctx.wib = ctx.alloc("wib", [128, T], BF16)
    ctx.stg = ctx.wrb
    ctx.VALS = ctx.alloc("VALS", [128, NK * T], BF16)
    ctx.y_tmp32 = ctx.alloc("y32", [128, 4096], F32)
    for e in (ctx.t, ctx.s, ctx.v):
        ld.wait(e)


def emit_l1ab_b(ctx, x_hbm, y_hbm, b, wqkv_hbm):
    nc, t, v, s, sp, g = ctx.nc, ctx.t, ctx.v, ctx.s, ctx.sp, ctx.g
    mm = ctx.sc("abf_mm")
    ev = ctx.sc("abf_ev")
    vv = ctx.sc("abf_vv")
    hp = ctx.sc("abf_hp")
    wld = ctx.sc("abf_wld")
    A, B, C, Dp, E, F, G = ctx.pl

    def vstep(dst, in0, in1, op):
        hp.inc(v.tensor_tensor(dst, in0, in1, op))
        hp.wait(v)

    for h in range(8):
        mm.wait(g)
        for i3, c0 in enumerate((h * 128, 1024 + h * 128, 2048 + h * 128)):
            for kt in range(8):
                wld.inc(g.dma_start(out=ctx.whd[:, i3 * 1024 + kt * 128: i3 * 1024 + kt * 128 + 128],
                                    in_=wqkv_hbm[kt * 128:(kt + 1) * 128, c0:c0 + 128]), 16)
        for i3, (pr_, pi_) in enumerate(((A, B), (C, Dp), (E, F))):
            for tg in range(4):
                bk = ctx.bank()
                ps = bk.begin(t)
                wld.wait(t)
                for kt in range(8):
                    ins = t.matmul(ps[:], ctx.whd[:, i3 * 1024 + kt * 128: i3 * 1024 + kt * 128 + 128],
                                   ctx.xn[:, kt * T + tg * 512: kt * T + tg * 512 + 512],
                                   start=(kt == 0), stop=(kt == 7))
                mm.inc(ins)
                mm.wait(s)
                bk.end(s.activation(ctx.stg[:, tg * 512:(tg + 1) * 512], ps[:], AF.Identity), ev)
                ev.wait(t)
                for (mat, dstp) in ((ctx.C128, pr_), (ctx.S128, pi_)):
                    bk2 = ctx.bank()
                    ps2 = bk2.begin(t)
                    mm.inc(t.matmul(ps2[:], mat[:], ctx.stg[:, tg * 512:(tg + 1) * 512],
                                    start=True, stop=True))
                    mm.wait(v)
                    hp.wait(v)
                    bk2.end(v.tensor_copy(dstp[:, tg * 512:(tg + 1) * 512], ps2[:]), vv)
        vv.wait(v)
        mm.wait(v)
        vstep(G[:], C[:], E[:], ALU.mult)
        vstep(E[:], Dp[:], E[:], ALU.mult)
        vstep(C[:], C[:], F[:], ALU.mult)
        vstep(F[:], Dp[:], F[:], ALU.mult)
        vstep(G[:], G[:], F[:], ALU.subtract)
        vstep(C[:], C[:], E[:], ALU.add)
        hp.inc(v.tensor_tensor_scan(Dp[:], ctx.ones2k[:], G[:], 0.0, ALU.mult, ALU.add))
        hp.wait(v)
        hp.inc(v.tensor_tensor_scan(E[:], ctx.ones2k[:], C[:], 0.0, ALU.mult, ALU.add))
        hp.wait(v)
        vstep(F[:], Dp[:], A[:], ALU.mult)
        vstep(G[:], E[:], B[:], ALU.mult)
        vstep(ctx.wrb[:], F[:], G[:], ALU.add)
        vstep(F[:], Dp[:], B[:], ALU.mult)
        vstep(G[:], E[:], A[:], ALU.mult)
        vstep(ctx.wib[:], F[:], G[:], ALU.subtract)
        hp.wait(t)
        for tg in range(4):
            bk = ctx.bank()
            ps = bk.begin(t)
            t.matmul(ps[:], ctx.C128[:], ctx.wrb[:, tg * 512:(tg + 1) * 512], start=True, stop=False)
            mm.inc(t.matmul(ps[:], ctx.S128n[:], ctx.wib[:, tg * 512:(tg + 1) * 512], start=False, stop=True))
            mm.wait(s)
            bk.end(s.activation(ctx.VALS[:, h * T + tg * 512: h * T + tg * 512 + 512],
                                ps[:], AF.Identity, scale=1.0 / 128.0), ev)
    ev.wait(t)
    for tg in range(4):
        for ot in range(8):
            bk = ctx.bank()
            ps = bk.begin(t)
            for kt in range(8):
                ins = t.matmul(ps[:], ctx.WOUT[:, kt * 1024 + ot * 128: kt * 1024 + ot * 128 + 128],
                               ctx.VALS[:, kt * T + tg * 512: kt * T + tg * 512 + 512],
                               start=(kt == 0), stop=(kt == 7))
            mm.inc(ins)
            mm.wait(s)
            bk.end(s.activation(ctx.y_tmp32[:, ot * 512:(ot + 1) * 512], ps[:], AF.Identity), ev)
        ev.wait(t)
        ev.wait(v)
        emit_finale_tg32(ctx, x_hbm, y_hbm, b, tg, ctx.y_tmp32)


def build_l1ab_program(debug=False):
    nc = bass.Bass("TRN2", target_bir_lowering=False, debug=False)
    x_in = nc.dram_tensor("x_in", [NB, T, D], F32, kind="ExternalInput").ap()
    x_out = nc.dram_tensor("x_out", [NB, T, D], F32, kind="ExternalOutput").ap()
    wqkv = nc.dram_tensor("w_qkv", [1024, 3072], F32, kind="ExternalInput").ap()
    wout = nc.dram_tensor("w_out", [1024, 1024], F32, kind="ExternalInput").ap()
    ctx = Ctx(nc)
    with nc.Block("init", no_gpsimd_drain=True) as blk:
        @blk.sync
        def _(sp):
            ctx.emit_const_init()
            emit_l1ab_weights(ctx, {"w_out": wout})
    for b in range(NB):
        with nc.Block(f"b{b}", no_gpsimd_drain=True) as blk:
            @blk.sync
            def _(sp, b=b):
                emit_ln(ctx, x_in, b)
                ctx.sc("ln_s").wait(ctx.t)
                ctx.sc("ln_tr").wait(ctx.t)
                emit_l1ab_b(ctx, x_in, x_out, b, wqkv)
    with nc.Block("fin", no_gpsimd_drain=True) as blk:
        @blk.sync
        def _(sp):
            ctx.sc("ystore").wait(ctx.sp)
    return nc


def emit_l1aa_weights(ctx, wt):
    nc, t, v, s, sp, g = ctx.nc, ctx.t, ctx.v, ctx.s, ctx.sp, ctx.g
    ld = ctx.sc("w_ld")
    ctx.WO = ctx.alloc("WOsb", [128, 8192], BF16)
    for kt in range(8):
        ld.inc(g.dma_start(out=ctx.WO[:, kt * 1024:(kt + 1) * 1024],
                           in_=wt["WO"][kt * 128:(kt + 1) * 128, :]), 16)
    ctx.bias_t = load_bias_cols(ctx, "bias_t", wt["bO"])
    ctx.bq_t = ctx.alloc("bq_t", [64, 16], F32)
    ctx.bk_t = ctx.alloc("bk_t", [64, 16], F32)
    ctx.bv_t = ctx.alloc("bv_t", [64, 16], F32)
    for bt, nm in ((ctx.bq_t, "bQ"), (ctx.bk_t, "bK"), (ctx.bv_t, "bV")):
        for hh in range(16):
            srcb = wt[nm][hh * 64:(hh + 1) * 64].rearrange("(p one) -> p one", one=1)
            ld.inc(sp.dma_start(out=bt[0:64, hh:hh + 1], in_=srcb), 16)
    ctx.whd = ctx.alloc("whd", [128, 3 * 512], BF16)
    ctx.SST = ctx.alloc("SST", [128, 16 * T], BF16)
    ctx.QH = ctx.alloc("QH", [128, 2048], BF16)
    ctx.KH = ctx.alloc("KH", [128, 2048], BF16)
    ctx.VH = ctx.alloc("VH", [128, 16 * 65], BF16)
    ctx.ATT = ctx.alloc("ATT", [128, NK * T], BF16)
    ctx.NCall = ctx.alloc("NCall", [128, 16], F32)
    ctx.ncacc = ctx.alloc("ncacc", [128, 4], F32)
    ctx.Crow = ctx.alloc("Crow", [128, 16], F32)
    ctx.w1 = ctx.alloc("w1", [128, 512], F32)
    ctx.w2 = ctx.alloc("w2", [128, 512], F32)
    ctx.sqx = ctx.alloc("sqx", [128, 512], BF16)
    ctx.q2row = ctx.alloc("q2row", [1, 512], F32)
    ctx.k2bf = ctx.alloc("k2bf", [1, 2048], BF16)
    ctx.onesrow = ctx.alloc("onesrow", [1, 2048], BF16)
    ctx.rec_row = ctx.alloc("rec_row", [1, 512], F32)
    ctx.rec_bf = ctx.alloc("rec_bf", [1, 512], BF16)
    ctx.att_s = ctx.alloc("att_s", [128, 512], BF16)
    ctx.att_d = ctx.alloc("att_d", [128, 512], BF16)
    ctx.ones_bf = ctx.alloc("ones_bf", [128, 64], BF16)
    g.memset(ctx.ones_bf[:], 1.0)
    g.memset(ctx.onesrow[:], 1.0)
    g.memset(ctx.KH[64:65, :], 1.0)
    ld.inc(g.memset(ctx.sqx[:], 0.0))
    gi = ctx.sc("w_g2")
    gi.wait(sp)
    gi.inc(sp.dma_start(out=ctx.QH[65:66, :], in_=ctx.onesrow[:]), 16)
    for e in (ctx.t, ctx.s, ctx.v):
        ld.wait(e); gi.wait(e); ctx.sc("bias_ld").wait(e)


def emit_l1aa_b(ctx, x_hbm, y_hbm, b, wq_hbm, wk_hbm, wv_hbm):
    nc, t, v, s, sp, g = ctx.nc, ctx.t, ctx.v, ctx.s, ctx.sp, ctx.g
    mm = ctx.sc("a1_mm")
    ev = ctx.sc("a1_ev")
    vv = ctx.sc("a1_vv")
    sc_s = ctx.sc("a1_s")
    sc_av = ctx.sc("a1_av")
    wld = ctx.sc("a1_wld")
    k2d = ctx.sc("a1_k2d")
    for h in range(16):
        mm.wait(g)
        for i3, w_hbm in enumerate((wq_hbm, wk_hbm, wv_hbm)):
            for kt in range(8):
                wld.inc(g.dma_start(out=ctx.whd[:, i3 * 512 + kt * 64: i3 * 512 + kt * 64 + 64],
                                    in_=w_hbm[kt * 128:(kt + 1) * 128, h * 64:(h + 1) * 64]), 16)
        sc_av.wait(s)
        sc_av.wait(v)
        for i3, nm in ((0, "q"), (1, "k")):
            dstT = ctx.QH if nm == "q" else ctx.KH
            bt = ctx.bq_t if nm == "q" else ctx.bk_t
            for tg in range(4):
                bk = ctx.bank()
                ps = bk.begin(t)
                wld.wait(t)
                for kt in range(8):
                    ins = t.matmul(ps[0:64, :], ctx.whd[:, i3 * 512 + kt * 64: i3 * 512 + kt * 64 + 64],
                                   ctx.xn[:, kt * T + tg * 512: kt * T + tg * 512 + 512],
                                   start=(kt == 0), stop=(kt == 7))
                mm.inc(ins)
                mm.wait(s)
                if nm == "q":
                    s.activation(dstT[0:64, tg * 512:(tg + 1) * 512], ps[0:64, :], AF.Identity,
                                 bias=bt[0:64, h:h + 1])
                else:
                    s.activation(ctx.w1[0:64, :], ps[0:64, :], AF.Identity, bias=bt[0:64, h:h + 1])
                    s.activation(dstT[0:64, tg * 512:(tg + 1) * 512], ctx.w1[0:64, :], AF.Identity,
                                 scale=-2.0)
                ev.wait(v)
                bk.release_at(ev)
                ev.inc(s.activation(ctx.sqx[0:64, :], dstT[0:64, tg * 512:(tg + 1) * 512], AF.Square))
                ev.wait(t)
                bq2 = ctx.bank()
                ps2 = bq2.begin(t)
                mm.inc(t.matmul(ps2[0:1, 0:512], ctx.ones_bf[0:64, 0:1], ctx.sqx[0:64, :],
                                start=True, stop=True))
                mm.wait(v)
                vv.wait(v)
                if nm == "q":
                    bq2.end(v.tensor_copy(ctx.QH[64:65, tg * 512:(tg + 1) * 512], ps2[0:1, 0:512]), vv)
                else:
                    bq2.end(v.tensor_scalar(ctx.k2bf[0:1, tg * 512:(tg + 1) * 512], ps2[0:1, 0:512],
                                            0.25, None, ALU.mult), vv)
        vv.wait(sp)
        ev.wait(sp)
        k2d.inc(sp.dma_start(out=ctx.KH[65:66, :], in_=ctx.k2bf[:]), 16)
        k2d.wait(t)
        ev.wait(t)
        vv.wait(t)
        for jt in range(16):
            for ig in range(4):
                bs = ctx.bank()
                pss = bs.begin(t)
                sc_s.wait(t)
                mm.inc(t.matmul(pss[:], ctx.KH[0:66, jt * 128: jt * 128 + 128],
                                ctx.QH[0:66, ig * 512: ig * 512 + 512], start=True, stop=True))
                mm.wait(s)
                vv.wait(s)
                s.activation(ctx.w1[:], pss[:], AF.Ln, scale=1.0 / 4096.0)
                bs.release_at(ctx._sc["a1_ev"]) if False else None
                bs.end(s.activation(ctx.w2[:], ctx.w1[:], AF.Exp, scale=0.5), ev)
                s.activation(ctx.w1[:], ctx.w2[:], AF.Ln, bias=ctx.onesf[:])
                sc_s.inc(s.activation(ctx.SST[:, jt * T + ig * 512: jt * T + ig * 512 + 512],
                                      ctx.w1[:], AF.Exp, scale=-65.5,
                                      accum_out=ctx.ncacc[:, ig:ig + 1]))
            sc_s.inc(s.add(ctx.stat[:, 0:1], ctx.ncacc[:, 3:4], 0.0))
            sc_s.wait(v)
            vv.inc(v.tensor_reduce(ctx.NCall[:, jt:jt + 1], ctx.ncacc[:], AX.X, ALU.add))
        vv.wait(s)
        s.activation(ctx.Crow[:], ctx.NCall[:], AF.Ln)
        ev.inc(s.activation(ctx.Crow[:], ctx.Crow[:], AF.Exp, scale=-0.5))
        ev.wait(t)
        ev.wait(v)
        for half in range(2):
            bkv = ctx.bank()
            psv = bkv.begin(t)
            for j8 in range(8):
                tt = half * 8 + j8
                for kt in range(8):
                    ins = t.matmul(psv[:, j8 * 64:(j8 + 1) * 64],
                                   ctx.xn[:, kt * T + tt * 128: kt * T + tt * 128 + 128],
                                   ctx.whd[:, 1024 + kt * 64: 1024 + kt * 64 + 64],
                                   start=(kt == 0), stop=(kt == 7))
            mm.inc(ins)
            mm.wait(v)
            last = None
            for j8 in range(8):
                tt = half * 8 + j8
                v.tensor_scalar(ctx.VH[:, tt * 65: tt * 65 + 64], psv[:, j8 * 64:(j8 + 1) * 64],
                                ctx.Crow[:, tt:tt + 1], None, ALU.mult)
                last = v.tensor_scalar(ctx.VH[:, tt * 65 + 64: tt * 65 + 65], ctx.Crow[:, tt:tt + 1],
                                       1.0, None, ALU.mult)
            bkv.end(last, vv)
        vv.wait(t)
        sc_s.wait(t)
        for ig in range(4):
            bo = ctx.hold[0]
            po = bo.begin(t)
            for jt in range(16):
                sc_av.inc(t.matmul(po[0:65, :], ctx.VH[:, jt * 65: jt * 65 + 65],
                                   ctx.SST[:, jt * T + ig * 512: jt * T + ig * 512 + 512],
                                   start=(jt == 0), stop=(jt == 15)))
            sc_av.wait(v)
            vv.wait(v)
            vv.inc(v.reciprocal(ctx.rec_row[:], po[64:65, :]))
            vv.wait(v)
            vv.inc(v.tensor_scalar(ctx.rec_bf[:], ctx.rec_row[:], 1.0, None, ALU.mult))
            vv.wait(s)
            ev.inc(s.activation(ctx.att_s[0:64, :], po[0:64, :], AF.Identity))
            bo.release_at(ev)
            ev.wait(t)
            vv.wait(t)
            br = ctx.bank()
            pr = br.begin(t)
            mm.inc(t.matmul(pr[0:64, :], ctx.ones_bf[0:1, 0:64], ctx.rec_bf[:], start=True, stop=True))
            mm.wait(v)
            ev.wait(v)
            br.end(v.tensor_tensor(ctx.att_d[0:64, :], ctx.att_s[0:64, :], pr[0:64, :], ALU.mult), vv)
            vv.wait(s)
            dstp = (h % 2) * 64
            dst = ctx.ATT[dstp:dstp + 64, (h // 2) * T + ig * 512: (h // 2) * T + ig * 512 + 512]
            ev.inc(s.activation(dst, ctx.att_d[0:64, :], AF.Identity, bias=ctx.bv_t[0:64, h:h + 1]))
    ev.wait(t)
    for tg in range(4):
        for ot in range(8):
            bk = ctx.bank()
            ps = bk.begin(t)
            for kt in range(8):
                ins = t.matmul(ps[:], ctx.WO[:, kt * 1024 + ot * 128: kt * 1024 + ot * 128 + 128],
                               ctx.ATT[:, kt * T + tg * 512: kt * T + tg * 512 + 512],
                               start=(kt == 0), stop=(kt == 7))
            mm.inc(ins)
            mm.wait(s)
            bk.end(s.activation(ctx.y_tmp[:, ot * 512:(ot + 1) * 512], ps[:], AF.Identity,
                                bias=ctx.bias_t[:, ot:ot + 1]), ev)
        ev.wait(t)
        ev.wait(v)
        emit_finale_tg(ctx, x_hbm, y_hbm, b, tg, ctx.y_tmp)


def build_l1aa_program(debug=False):
    nc = bass.Bass("TRN2", target_bir_lowering=False, debug=False)
    x_in = nc.dram_tensor("x_in", [NB, T, D], F32, kind="ExternalInput").ap()
    x_out = nc.dram_tensor("x_out", [NB, T, D], F32, kind="ExternalOutput").ap()
    wt = {}
    for nm in ("WQ", "WK", "WV", "WO"):
        wt[nm] = nc.dram_tensor(nm, [1024, 1024], F32, kind="ExternalInput").ap()
    for nm in ("bQ", "bK", "bV", "bO"):
        wt[nm] = nc.dram_tensor(nm, [1024], F32, kind="ExternalInput").ap()
    ctx = Ctx(nc)
    with nc.Block("init", no_gpsimd_drain=True) as blk:
        @blk.sync
        def _(sp):
            ctx.emit_const_init()
            emit_l1aa_weights(ctx, wt)
    for b in range(NB):
        with nc.Block(f"b{b}", no_gpsimd_drain=True) as blk:
            @blk.sync
            def _(sp, b=b):
                emit_ln(ctx, x_in, b)
                ctx.sc("ln_s").wait(ctx.t)
                ctx.sc("ln_tr").wait(ctx.t)
                emit_l1aa_b(ctx, x_in, x_out, b, wt["WQ"], wt["WK"], wt["WV"])
    with nc.Block("fin", no_gpsimd_drain=True) as blk:
        @blk.sync
        def _(sp):
            ctx.sc("ystore").wait(ctx.sp)
    return nc


def build_fused_program(nphases=8):
    nc = bass.Bass("TRN2", target_bir_lowering=False, debug=False)
    x_in = nc.dram_tensor("x_in", [NB, T, D], F32, kind="ExternalInput").ap()
    x_out = nc.dram_tensor("x_out", [NB, T, D], F32, kind="ExternalOutput").ap()
    xs_a = nc.dram_tensor("xs_a", [NB, T, D], F32).ap()
    xs_b = nc.dram_tensor("xs_b", [NB, T, D], F32).ap()
    A_scr = nc.dram_tensor("A_scr", [128, 8192], BF16).ap()
    cT_scr = nc.dram_tensor("cT_scr", [128, 8192], BF16).ap()
    sP_scr = nc.dram_tensor("sP_scr", [128, 8192], BF16).ap()

    wt = {}
    for nm, shp in [("L0_attn_a_wq_lat", [1024, 256]), ("L0_attn_a_wk_lat", [1024, 256]),
                    ("L0_attn_a_wv_lat", [1024, 256]), ("L0_attn_a_wq_heads", [256, 1024]),
                    ("L0_attn_a_wk_heads", [256, 1024]), ("L0_attn_a_wv_heads", [256, 1024]),
                    ("L0_attn_a_w_proj", [1024, 1024]), ("L0_attn_a_b_proj", [1024]),
                    ("L0_attn_b_W_Q", [512, 1024]), ("L0_attn_b_W_K", [512, 1024]),
                    ("L0_attn_b_W_V", [512, 1024]), ("L0_attn_b_W_g_w", [512, 1024]),
                    ("L0_attn_b_W_g_b", [512]), ("L0_attn_b_W_O", [1024, 512]),
                    ("L0_attn_b_decay_log", [8]),
                    ("L0_ffn_w_freq_real", [1024, 1024]), ("L0_ffn_w_freq_imag", [1024, 1024]),
                    ("L0_ffn_bias", [1024]),
                    ("L1_attn_a_WQ", [1024, 1024]), ("L1_attn_a_bQ", [1024]),
                    ("L1_attn_a_WK", [1024, 1024]), ("L1_attn_a_bK", [1024]),
                    ("L1_attn_a_WV", [1024, 1024]), ("L1_attn_a_bV", [1024]),
                    ("L1_attn_a_WO", [1024, 1024]), ("L1_attn_a_bO", [1024]),
                    ("L1_attn_b_w_qkv", [1024, 3072]), ("L1_attn_b_w_out", [1024, 1024]),
                    ("L1_ffn_theta", [1024, 1024]), ("L1_ffn_phi", [1024, 1024]),
                    ("L1_ffn_theta_noise", [1024, 1024]), ("L1_ffn_phi_noise", [1024, 1024])]:
        wt[nm] = nc.dram_tensor(nm, shp, F32, kind="ExternalInput").ap()

    ctx = Ctx(nc)
    with nc.Block("init", no_gpsimd_drain=True) as blk:
        @blk.sync
        def _(sp):
            ctx.emit_const_init()

    stream = [x_in]
    for i in range(nphases - 1):
        stream.append(xs_a if i % 2 == 0 else xs_b)
    stream.append(x_out)

    def ph_l0aa(b, xi, xo):
        emit_l0aa_b(ctx, xi, xo, b)

    def ph_l0ffn(b, xi, xo):
        emit_l0ffn_b(ctx, xi, xo, b, ctx.bias_t)

    def ph_l0ab(b, xi, xo):
        emit_l0ab_b(ctx, xi, xo, b)

    def ph_l1aa(b, xi, xo):
        emit_l1aa_b(ctx, xi, xo, b, wt["L1_attn_a_WQ"], wt["L1_attn_a_WK"], wt["L1_attn_a_WV"])

    def ph_l1ffn(b, xi, xo):
        emit_l1ffn_b(ctx, xi, xo, b)

    def ph_l1ab(b, xi, xo):
        emit_l1ab_b(ctx, xi, xo, b, wt["L1_attn_b_w_qkv"])

    def w_l0aa():
        emit_l0aa_weights(ctx, {
            "wq_lat": wt["L0_attn_a_wq_lat"], "wk_lat": wt["L0_attn_a_wk_lat"],
            "wv_lat": wt["L0_attn_a_wv_lat"], "wq_heads": wt["L0_attn_a_wq_heads"],
            "wk_heads": wt["L0_attn_a_wk_heads"], "wv_heads": wt["L0_attn_a_wv_heads"],
            "w_proj": wt["L0_attn_a_w_proj"], "b_proj": wt["L0_attn_a_b_proj"]})

    def w_l0ffn(mode):
        def f():
            emit_build_A(ctx, wt["L0_ffn_w_freq_real"], wt["L0_ffn_w_freq_imag"],
                         cache=A_scr, mode=mode)
            ctx.bias_t = load_bias_cols(ctx, "bias_t", wt["L0_ffn_bias"])
            ctx.sc("bias_ld").wait(ctx.s)
        return f

    def w_l0ab():
        emit_l0ab_weights(ctx, {
            "W_Q": wt["L0_attn_b_W_Q"], "W_K": wt["L0_attn_b_W_K"], "W_V": wt["L0_attn_b_W_V"],
            "W_g_w": wt["L0_attn_b_W_g_w"], "W_g_b": wt["L0_attn_b_W_g_b"],
            "W_O": wt["L0_attn_b_W_O"], "decay_log": wt["L0_attn_b_decay_log"]})

    def w_l1aa():
        emit_l1aa_weights(ctx, {
            "WQ": wt["L1_attn_a_WQ"], "WK": wt["L1_attn_a_WK"], "WV": wt["L1_attn_a_WV"],
            "WO": wt["L1_attn_a_WO"], "bQ": wt["L1_attn_a_bQ"], "bK": wt["L1_attn_a_bK"],
            "bV": wt["L1_attn_a_bV"], "bO": wt["L1_attn_a_bO"]})

    def w_l1ffn(mode):
        def f():
            emit_l1ffn_weights(ctx, {"theta": wt["L1_ffn_theta"], "phi": wt["L1_ffn_phi"],
                                     "theta_noise": wt["L1_ffn_theta_noise"],
                                     "phi_noise": wt["L1_ffn_phi_noise"]},
                               cache=(cT_scr, sP_scr), mode=mode)
        return f

    def w_l1ab():
        emit_l1ab_weights(ctx, {"w_out": wt["L1_attn_b_w_out"]})

    plan = [("q0", w_l0aa, ph_l0aa), ("q1", w_l0ffn("build"), ph_l0ffn),
            ("q2", w_l0ab, ph_l0ab), ("q3", w_l0ffn("load"), ph_l0ffn),
            ("q4", w_l1aa, ph_l1aa), ("q5", w_l1ffn("build"), ph_l1ffn),
            ("q6", w_l1ab, ph_l1ab), ("q7", w_l1ffn("load"), ph_l1ffn)][:nphases]

    for i, (pref, wfn, pfn) in enumerate(plan):
        ctx.begin_phase(pref + "_")
        with nc.Block(pref, no_gpsimd_drain=True) as blk:
            @blk.sync
            def _(sp, i=i, wfn=wfn, pfn=pfn):
                ctx.sc("ystore").wait(ctx.sp)
                wfn()
                for b in range(NB):
                    emit_ln(ctx, stream[i], b)
                    ctx.sc("ln_s").wait(ctx.t)
                    ctx.sc("ln_tr").wait(ctx.t)
                    pfn(b, stream[i], stream[i + 1])
        ctx.end_phase()

    with nc.Block("fin", no_gpsimd_drain=True) as blk:
        @blk.sync
        def _(sp):
            ctx.sc("ystore").wait(ctx.sp)
    return nc



_FUSED = [None]

def kernel(**inputs):
    from concourse.bass_utils import run_bass_kernel_spmd
    if _FUSED[0] is None:
        _FUSED[0] = build_fused_program(nphases=8)
    nc = _FUSED[0]
    cons = host_constants()
    n = 8
    in_maps = []
    for c in range(n):
        im = dict(cons)
        for k, v_ in inputs.items():
            if k != "L0_attn_a_x":
                im[k] = np.asarray(v_)
        im["x_in"] = np.ascontiguousarray(inputs["L0_attn_a_x"][4 * c:4 * c + 4])
        in_maps.append(im)
    res = run_bass_kernel_spmd(nc, in_maps, core_ids=list(range(n)))
    return np.concatenate([np.asarray(res.results[c]["x_out"]) for c in range(n)], axis=0)
